# version 04
# baseline (speedup 1.0000x reference)
import contextlib
import numpy as np
import ml_dtypes
import concourse.bass as bass
import concourse.mybir as mybir
from concourse.bass_utils import run_bass_kernel_spmd

F32 = mybir.dt.float32
BF16 = mybir.dt.bfloat16
I32 = mybir.dt.int32
AF = mybir.ActivationFunctionType
ALU = mybir.AluOpType
AX = mybir.AxisListType
BFD = ml_dtypes.bfloat16

B, S, D = 2, 4096, 2048
H, NKV = 16, 4
CQ, QN, QR = 128, 96, 32
CKV, KR = 128, 64
KN, VD = 64, 256
THETA = 1e4
EPS = 1e-5
CSC = float(1.0 / np.sqrt(128.0))
NEGB_MUL = float(-np.sqrt(128.0))
P = 128
NT = S // P
OWN = 8
IFF = 2 * D

ENGINES = ("sync", "tensor", "vector", "scalar", "gpsimd")


class Kern:
    ENGSEM = {"vector": "s_dve", "scalar": "s_act", "gpsimd": "s_pool"}

    def __init__(self, nc):
        self.nc = nc
        self.plan = {e: [] for e in ENGINES}
        self.hwm = {e: {} for e in ENGINES}
        self.cnt = {}
        self.sems = {}
        self.tok = {}
        self.prev = {}

    def add_sem(self, ctx, name):
        self.sems[name] = ctx.enter_context(self.nc.semaphore(name))
        self.cnt[name] = 0

    def emit(self, eng, fn, waits=(), inc=None, inc_by=1, chain=True):
        flat = []
        for w in waits:
            if w is None:
                continue
            if isinstance(w, list):
                flat.extend(x for x in w if x is not None)
            else:
                flat.append(w)
        if chain and eng in self.ENGSEM and inc_by == 1:
            es = self.ENGSEM[eng]
            if self.prev.get(eng) is not None:
                flat.append(self.prev[eng])
            if inc is None:
                inc = es
        t = None
        if inc is not None:
            self.cnt[inc] += inc_by
            t = (inc, self.cnt[inc])
        if chain and eng in self.ENGSEM and inc_by == 1:
            self.prev[eng] = t
        self.plan[eng].append((tuple(flat), fn, inc, inc_by, t))
        return t

    def flush(self, block):
        for eng in ENGINES:
            ops = self.plan[eng]
            if not ops:
                continue
            hwm = self.hwm[eng]
            sems = self.sems

            def body(e, ops=ops, hwm=hwm, sems=sems):
                for waits, fn, inc, inc_by, t in ops:
                    for (sn, val) in waits:
                        if hwm.get(sn, 0) < val:
                            e.wait_ge(sems[sn], val)
                            hwm[sn] = val
                    last = fn(e)
                    if inc is not None:
                        assert last is not None
                        last.then_inc(sems[inc], inc_by)

            getattr(block, eng)(body)
        self.plan = {e: [] for e in ENGINES}

    def dma(self, q, dst, src, waits=(), sem=None):
        sem = sem or {"sync": "s_ld", "scalar": "s_l2", "gpsimd": "s_st"}[q]
        prior = (sem, self.cnt[sem])
        return self.emit(q, lambda e: e.dma_start(out=dst, in_=src),
                         tuple(waits) + (prior,), inc=sem, inc_by=16)

    def mm(self, ps, lhsT, rhs, start, stop, waits=(), inc=None):
        return self.emit(
            "tensor",
            lambda e: e.matmul(ps, lhsT, rhs, start=start, stop=stop),
            waits, inc=inc)

    def tr(self, ps, src, iden, waits=(), inc=None):
        return self.emit("tensor", lambda e: e.transpose(ps, src, iden),
                         waits, inc=inc)

    def cp(self, eng, dst, src, waits=(), inc=None):
        return self.emit(eng, lambda e: e.tensor_copy(dst, src), waits,
                         inc=inc)

    def act(self, dst, src, func, waits=(), inc=None, scale=None, bias=None,
            accum_out=None):
        kw = {}
        if scale is not None:
            kw["scale"] = scale
        if bias is not None:
            kw["bias"] = bias
        if accum_out is not None:
            kw["accum_out"] = accum_out
        return self.emit("scalar",
                         lambda e: e.activation(dst, src, func, **kw),
                         waits, inc=inc)

    def ts(self, eng, dst, src, s1, s2, op0, op1, waits=(), inc=None):
        if op1 is None:
            return self.emit(
                eng,
                lambda e: e.tensor_scalar(dst, src, s1, None, op0=op0),
                waits, inc=inc)
        return self.emit(
            eng,
            lambda e: e.tensor_scalar(dst, src, s1, s2, op0=op0, op1=op1),
            waits, inc=inc)

    def tt(self, eng, dst, a, b, op, waits=(), inc=None):
        return self.emit(eng, lambda e: e.tensor_tensor(dst, a, b, op=op),
                         waits, inc=inc)

    def stt(self, eng, dst, a, sc, b, op0, op1, waits=(), inc=None):
        return self.emit(
            eng,
            lambda e: e.scalar_tensor_tensor(dst, a, sc, b, op0=op0, op1=op1),
            waits, inc=inc)

    def barrier(self):
        toks = [(sn, self.cnt[sn]) for sn in self.cnt]
        for eng in ENGINES:
            self.emit(eng, lambda e: None, waits=tuple(toks), chain=False)

    def recip(self, dst, src, waits=(), inc=None):
        return self.emit("vector", lambda e: e.reciprocal(dst, src),
                         waits, inc=inc)

    def red(self, dst, src, op, absval=False, waits=(), inc=None):
        if absval:
            return self.emit(
                "vector",
                lambda e: e.tensor_reduce(dst, src, axis=AX.X, op=op,
                                          apply_absolute_value=True),
                waits, inc=inc)
        return self.emit(
            "vector",
            lambda e: e.tensor_reduce(dst, src, axis=AX.X, op=op),
            waits, inc=inc)


def ap3(tensor, off, dims):
    return bass.AP(tensor, off, dims)


def build_program(stop=99, n1=NT, parts=65535, lnlvl=9, dbg=False):
    nc = bass.Bass("TRN2", target_bir_lowering=False,
                   detect_race_conditions=False)

    def din(name, shape, dtype=F32):
        return nc.dram_tensor(name, list(shape), dtype,
                              kind="ExternalInput").ap()

    x_seq = din("x_seq", (S, D))
    x_own = din("x_own", (OWN * P, D))
    wdq_t = din("wdq_t", (P, 16 * CQ))
    wdkv_t = din("wdkv_t", (P, 16 * (CKV + KR)))
    wuq_p = din("wuq_p", (CQ, 2048))
    wukv_i = din("wukv_i", (CKV, 1280))
    w_o = din("w_o", (H * VD, D))
    w_up = din("w_up", (IFF, D))
    w_down = din("w_down", (D, IFF))
    gq_bc = din("gq_bc", (P, CQ))
    gkv_bc = din("gkv_bc", (P, CKV))
    ropeKc = din("ropeKc", (P, NT * KR))
    ropeKs = din("ropeKs", (P, NT * KR))
    ropeQc = din("ropeQc", (OWN * P, H * QR))
    ropeQs = din("ropeQs", (OWN * P, H * QR))
    mask4 = din("mask4", (P, 4 * 512), BF16)
    iden_f = din("iden_f", (P, P), BF16)
    iden_f32 = din("iden_f32", (P, P))
    ones_cb = din("ones_cb", (P, 1), BF16)
    ones_rb = din("ones_rb", (1, P), BF16)
    ones_cf = din("ones_cf", (P, 1))
    ones_rf = din("ones_rf", (1, P))

    out_own = nc.dram_tensor("out_own", [OWN * P, D], F32,
                             kind="ExternalOutput").ap()
    dbg_x1 = dbg_of = dbg_wu = dbg_sc = None
    dbg_kv = None
    if dbg:
        dbg_sc = nc.dram_tensor("dbg_sc", [P, 66], F32,
                                kind="ExternalOutput").ap()
        dbg_kv = nc.dram_tensor("dbg_kv", [P, 3 * 8192], BF16,
                                kind="ExternalOutput").ap()
        dbg_x1 = nc.dram_tensor("dbg_x1", [OWN * P, D], F32,
                                kind="ExternalOutput").ap()
        dbg_of = nc.dram_tensor("dbg_of", [OWN * P, H * VD], BF16,
                                kind="ExternalOutput").ap()
        dbg_wu = nc.dram_tensor("dbg_wu", [D, IFF], BF16,
                                kind="ExternalOutput").ap()
    outf_dram = nc.dram_tensor("outf_dram", [OWN * P, H * VD], BF16,
                               kind="Internal").ap()
    x1_dram = nc.dram_tensor("x1_dram", [OWN * P, D], F32,
                             kind="Internal").ap()
    wo_bf_dram = nc.dram_tensor("wo_bf_dram", [H * VD, D], BF16,
                                kind="Internal").ap()
    wupT_d = nc.dram_tensor("wupT_d", [D, IFF], BF16, kind="Internal")
    wdnT_d = nc.dram_tensor("wdnT_d", [IFF, D], BF16, kind="Internal")

    with contextlib.ExitStack() as top:
        k = Kern(nc)
        for sname in ("s_ld", "s_l2", "s_st", "s_pe", "s_dve", "s_act",
                      "s_pool", "s_x0", "s_x1", "s_stg", "s_wq", "s_wo0",
                      "s_wo1", "s_of"):
            k.add_sem(top, sname)

        def mk(ctx, name, shape, dtype):
            return ctx.enter_context(nc.sbuf_tensor(name, list(shape), dtype))

        iden = mk(top, "iden", (P, P), BF16)
        idenf = mk(top, "idenf", (P, P), F32)
        onescb = mk(top, "onescb", (P, 1), BF16)
        onesrb = mk(top, "onesrb", (1, P), BF16)
        onescf = mk(top, "onescf", (P, 1), F32)
        onesrf = mk(top, "onesrf", (1, P), F32)
        vec = mk(top, "vec", (P, 32), F32)
        sca = mk(top, "sca", (1, 32), F32)
        wsum = mk(top, "wsum", (P, 64), F32)
        negB = mk(top, "negB", (P, 1), F32)
        supv = mk(top, "supv", (P, 1), F32)
        rsupv = mk(top, "rsupv", (P, 1), F32)
        sdnv = mk(top, "sdnv", (P, 1), F32)
        rsdnv = mk(top, "rsdnv", (P, 1), F32)
        srow1 = mk(top, "srow1", (P, OWN), F32)
        srow2 = mk(top, "srow2", (P, OWN), F32)
        dumpb = mk(top, "dumpb", (P, D), BF16)

        blk = top.enter_context(nc.Block())

        k.emit("gpsimd", lambda e: e.memset(vec[:], 0.0), inc="s_pool")
        t_vz = ("s_pool", k.cnt["s_pool"])
        k.emit("gpsimd", lambda e: e.memset(sca[:], 0.0), inc="s_pool")
        t_sz = ("s_pool", k.cnt["s_pool"])

        vecfree = [t_vz]

        def ln_chain(src_ps_or_sb, n, gamma_bc, out_dst, mmt, extra=(),
                     app_extra=()):
            if lnlvl == 16:
                tsq = k.act(dumpb[:, 0:min(n, D)], src_ps_or_sb, AF.Square,
                            waits=(mmt,), inc="s_act")
                vecfree[0] = tsq
                return tsq, tsq, tsq
            tred = k.red(vec[:, 0:1], src_ps_or_sb, ALU.add,
                         waits=(mmt, vecfree[0]) + tuple(extra),
                         inc="s_dve")
            if lnlvl < 1:
                vecfree[0] = tred
                return tred, tred, tred
            if lnlvl == 11:
                tsq = k.act(dumpb[:, 0:min(n, D)], src_ps_or_sb, AF.Square,
                            waits=(mmt, tred), inc="s_act")
                vecfree[0] = tsq
                return tred, tsq, tred
            if lnlvl == 12:
                tsq = k.act(dumpb[:, 0:min(n, D)], src_ps_or_sb, AF.Copy,
                            accum_out=vec[:, 1:2], waits=(mmt, tred),
                            inc="s_act")
                vecfree[0] = tsq
                return tred, tsq, tred
            tsq = k.act(dumpb[:, 0:min(n, D)], src_ps_or_sb, AF.Square,
                        accum_out=vec[:, 1:2], waits=(mmt, tred),
                        inc="s_act")
            if lnlvl < 2:
                vecfree[0] = tsq
                return tred, tsq, tred
            k.ts("vector", vec[:, 2:3], vec[:, 0:1], 1.0 / n, None,
                 ALU.mult, None)
            k.ts("vector", vec[:, 3:4], vec[:, 1:2], 1.0 / n, None,
                 ALU.mult, None, waits=(tsq,))
            k.tt("vector", vec[:, 4:5], vec[:, 2:3], vec[:, 2:3], ALU.mult)
            tvar = k.stt("vector", vec[:, 4:5], vec[:, 3:4], float(EPS),
                         vec[:, 4:5], ALU.add, ALU.subtract, inc="s_dve")
            if lnlvl < 3:
                vecfree[0] = tsq
                return tvar, tsq, tvar
            k.act(vec[:, 5:6], vec[:, 4:5], AF.Ln, waits=(tvar,))
            trst = k.act(vec[:, 5:6], vec[:, 5:6], AF.Exp, scale=-0.5,
                         inc="s_act")
            if lnlvl < 4:
                vecfree[0] = trst
                return tvar, trst, tvar
            tmr = k.tt("vector", vec[:, 6:7], vec[:, 2:3], vec[:, 5:6],
                       ALU.mult, waits=(trst,), inc="s_dve")
            if lnlvl < 5:
                vecfree[0] = trst
                return tvar, trst, tmr
            tapp = k.act(out_dst, src_ps_or_sb, AF.Copy, scale=vec[:, 5:6],
                         waits=(trst,) + tuple(app_extra), inc="s_act")
            tsub = k.ts("vector", out_dst, out_dst, vec[:, 6:7], None,
                        ALU.subtract, None, waits=(tapp, tmr), inc="s_dve")
            vecfree[0] = tapp
            return tvar, tapp, tsub

        s13 = contextlib.ExitStack()
        Kf = [mk(s13, f"kf{g}", (P, S), BF16) for g in range(NKV)]
        Vt = [mk(s13, f"vt{g}", (P, NT * VD), BF16) for g in range(NKV)]

        s12 = contextlib.ExitStack()
        wdqb = mk(s12, "wdqb", (P, 16 * CQ), BF16)
        wuqb = mk(s12, "wuqb", (CQ, 2048), BF16)
        xb = [mk(s12, f"xb{i}", (P, D), BF16) for i in range(2)]
        xfb = [mk(s12, f"xfb{i}", (P, D), BF16) for i in range(2)]
        cvn = mk(s12, "cvn", (P, P), BF16)

        tC = {}
        for nm, dst, src in [("iden", iden[:], iden_f),
                             ("idenf", idenf[:], iden_f32),
                             ("onescb", onescb[:], ones_cb),
                             ("onesrb", onesrb[:], ones_rb),
                             ("onescf", onescf[:], ones_cf),
                             ("onesrf", onesrf[:], ones_rf)]:
            tC[nm] = k.dma("gpsimd", dst, src)
        t_wdqb = k.dma("gpsimd", wdqb[:], wdq_t)
        t_wuqb = k.dma("gpsimd", wuqb[:], wuq_p)
        t_wocast = None
        if parts & 128:
            t_wocast = k.dma("gpsimd", wo_bf_dram, w_o)

        s1 = contextlib.ExitStack()
        wdkvb = mk(s1, "wdkvb", (P, 16 * (CKV + KR)), BF16)
        wukvb = mk(s1, "wukvb", (CKV, 1280), BF16)
        rkc = mk(s1, "rkc", (P, NT * KR), F32)
        rks = mk(s1, "rks", (P, NT * KR), F32)
        knsc = mk(s1, "knsc", (P, 320), BF16)
        scrA = mk(s1, "scrA1", (P, D), F32)
        scrB = mk(s1, "scrB1", (P, D), F32)
        gkvbc = mk(s1, "gkvbc", (P, CKV), F32)
        ckvf = mk(s1, "ckvf", (P, P), BF16)
        wlf = [mk(s1, f"wlf{i}", (P, D), F32) for i in range(2)]
        pdkv = [s1.enter_context(nc.psum_tensor(f"pdkv{i}", [P, 192], F32))
                for i in range(2)]
        pkv = [s1.enter_context(nc.psum_tensor("pkv0", [P, 512], F32)),
               s1.enter_context(nc.psum_tensor("pkv1", [P, 512], F32)),
               s1.enter_context(nc.psum_tensor("pkv2", [P, 256], F32))]
        ptr = [s1.enter_context(nc.psum_tensor(f"ptr{i}", [P, 1024], BF16))
               for i in range(2)]
        psc = s1.enter_context(nc.psum_tensor("psc", [P, 4], F32))

        if stop <= 0:
            k.flush(blk)
            s1.close(); s12.close(); s13.close()
            return nc
        t_wdkvb = k.dma("gpsimd", wdkvb[:], wdkv_t)
        t_wukvb = k.dma("gpsimd", wukvb[:], wukv_i)
        t_rkc = k.dma("gpsimd", rkc[:], ropeKc)
        t_rks = k.dma("gpsimd", rks[:], ropeKs)
        t_gkv = k.dma("gpsimd", gkvbc[:], gkv_bc)

        if not (parts & 1):
            k.emit("gpsimd", lambda e: e.memset(xfb[0][:], 0.01),
                   inc="s_pool")
            k.emit("gpsimd", lambda e: e.memset(xfb[1][:], 0.01),
                   inc="s_pool")
        t_xfmem = ("s_pool", k.cnt["s_pool"])
        tok_x_free = [None, None]
        tok_xfb_free = [None, None]
        tok_tt1 = None
        tok_tt4 = None
        tok_tgam_prev = None
        tok_dkv_free = [[], []]
        tok_kv_free = []
        tok_ptr_free = [None, None]
        tok_wl_free = [None, None]
        tok_kdone = None

        for t in range(n1):
            par = t % 2
            if not (parts & 1):
                txf = t_xfmem
                if not (parts & 2):
                    continue
                mmt = None
                for dc in range(16):
                    mmt = k.mm(pdkv[par][:],
                               xfb[par][:, dc * 128:(dc + 1) * 128],
                               wdkvb[:, dc * 192:(dc + 1) * 192],
                               start=(dc == 0), stop=(dc == 15),
                               waits=(txf, t_wdkvb, tok_dkv_free[par]),
                               inc="s_pe" if dc == 15 else None)
                tok_xfb_free[par] = mmt
            else:
                tx = k.dma("gpsimd", xb[par][:], x_seq[t * P:(t + 1) * P, :],
                           waits=(tok_x_free[par],), sem=f"s_x{par}")
                for half in range(2):
                    bank = ptr[1]
                    trt = None
                    for bnum in range(8):
                        dc = half * 8 + bnum
                        trt = k.tr(bank[:, bnum * 128:(bnum + 1) * 128],
                                   xb[par][:, dc * 128:(dc + 1) * 128],
                                   iden[:],
                                   waits=(tx, tC["iden"], tok_ptr_free[1]),
                                   inc="s_pe" if bnum == 7 else None)
                    tok_ptr_free[1] = k.cp(
                        "vector",
                        xfb[par][:, half * 1024:(half + 1) * 1024],
                        bank[:], waits=(trt, tok_xfb_free[par]),
                        inc="s_dve")
                tok_x_free[par] = ("s_pe", k.cnt["s_pe"])
                txf = tok_ptr_free[1]

                if not (parts & 2):
                    continue
                mmt = None
                for dc in range(16):
                    mmt = k.mm(pdkv[par][:],
                               xfb[par][:, dc * 128:(dc + 1) * 128],
                               wdkvb[:, dc * 192:(dc + 1) * 192],
                               start=(dc == 0), stop=(dc == 15),
                               waits=(txf, t_wdkvb, tok_dkv_free[par]),
                               inc="s_pe" if dc == 15 else None)
                tok_xfb_free[par] = mmt

            if not (parts & 4):
                continue
            ps = pdkv[par]
            tkr0 = k.cp("vector", scrA[:, 256:256 + KR],
                        ps[:, CKV:CKV + KR], waits=(mmt,))
            tvar, tapp, tsub = ln_chain(ps[:, 0:CKV], CKV, None,
                                        scrA[:, 0:CKV], mmt,
                                        app_extra=(tok_tgam_prev,))
            if lnlvl >= 11:
                tok_dkv_free[par] = [tapp, tvar]
                continue
            tgam = k.tt("vector", cvn[:], scrA[:, 0:CKV], gkvbc[:],
                        ALU.mult, waits=(t_gkv, tsub), inc="s_dve")

            if not (parts & 32):
                tok_dkv_free[par] = [tapp]
                continue
            k.cp("vector", ap3(scrA, 320, [[D, P], [2, KR // 2]]),
                 ap3(scrA, 257, [[D, P], [2, KR // 2]]))
            tsw = k.cp("vector", ap3(scrA, 321, [[D, P], [2, KR // 2]]),
                       ap3(scrA, 256, [[D, P], [2, KR // 2]]), inc="s_dve")
            k.tt("vector", scrB[:, 0:KR], scrA[:, 256:256 + KR],
                 rkc[:, t * KR:(t + 1) * KR], ALU.mult,
                 waits=(t_rkc, t_rks))
            k.tt("vector", scrB[:, 64:128], scrA[:, 320:320 + KR],
                 rks[:, t * KR:(t + 1) * KR], ALU.mult, waits=(tsw,))
            trr = k.tt("vector", knsc[:, 256:320], scrB[:, 0:64],
                       scrB[:, 64:128], ALU.add, waits=(tok_tt4,),
                       inc="s_dve")
            tok_dkv_free[par] = [trr, tapp]

            if not (parts & 8):
                continue
            tt1 = k.tr(ptr[0][:, 0:128], cvn[:], iden[:],
                       waits=(tgam, tok_ptr_free[0]), inc="s_pe")
            tok_tt1 = tt1
            tckvf = k.cp("vector", ckvf[:], ptr[0][:, 0:128], waits=(tt1,),
                         inc="s_dve")

            k.mm(pkv[0][:], ckvf[:], wukvb[:, 0:512], True, True,
                 waits=[tckvf, t_wukvb] + tok_kv_free)
            k.mm(pkv[1][:], ckvf[:], wukvb[:, 512:1024], True, True)
            tkv = k.mm(pkv[2][:], ckvf[:], wukvb[:, 1024:1280], True, True,
                       inc="s_pe")
            k.cp("vector", Vt[0][:, t * VD:(t + 1) * VD], pkv[0][:, 256:512],
                 waits=(tkv,))
            k.cp("vector", Vt[1][:, t * VD:(t + 1) * VD], pkv[1][:, 0:256])
            k.cp("vector", Vt[2][:, t * VD:(t + 1) * VD], pkv[1][:, 256:512])
            k.cp("vector", Vt[3][:, t * VD:(t + 1) * VD], pkv[2][:])
            tkn = k.cp("vector", knsc[:, 0:256], pkv[0][:, 0:256],
                       inc="s_dve")
            tok_kv_free = [tkn]
            k.tr(ptr[0][:, 128:256], knsc[:, 0:128], iden[:],
                 waits=(tkn, tckvf), inc="s_pe")
            k.tr(ptr[0][:, 256:384], knsc[:, 128:256], iden[:], inc="s_pe")
            tt4 = k.tr(ptr[0][0:64, 384:512], knsc[:, 256:320], iden[:],
                       waits=(trr,), inc="s_pe")
            tok_tt4 = tt4
            k.cp("vector", Kf[0][0:64, t * P:(t + 1) * P],
                 ptr[0][0:64, 128:256], waits=(tt4,))
            k.cp("vector", Kf[1][0:64, t * P:(t + 1) * P],
                 ptr[0][64:128, 128:256])
            k.cp("vector", Kf[2][0:64, t * P:(t + 1) * P],
                 ptr[0][0:64, 256:384])
            k.cp("vector", Kf[3][0:64, t * P:(t + 1) * P],
                 ptr[0][64:128, 256:384])
            for g in range(3):
                k.cp("vector", Kf[g][64:128, t * P:(t + 1) * P],
                     ptr[0][0:64, 384:512])
            tkall = k.cp("vector", Kf[3][64:128, t * P:(t + 1) * P],
                         ptr[0][0:64, 384:512], inc="s_dve")
            tok_ptr_free[0] = tkall
            tok_kdone = tkall

            if not (parts & 16):
                continue
            wpar = t % 2
            tu = k.dma("scalar", wlf[wpar][:], w_up[t * P:(t + 1) * P, :],
                       waits=(tok_wl_free[wpar],))
            tau = k.act(dumpb[:], wlf[wpar][:], AF.Abs,
                        accum_out=wsum[:, t:t + 1], waits=(tu,), inc="s_act")
            rr, cc_ = t % 16, t // 16
            td = k.dma("scalar", wlf[wpar][:],
                       w_down[rr * P:(rr + 1) * P, cc_ * D:(cc_ + 1) * D],
                       waits=(tau,))
            tad = k.act(dumpb[:], wlf[wpar][:], AF.Abs,
                        accum_out=wsum[:, 32 + t:33 + t], waits=(td,),
                        inc="s_act")
            tok_wl_free[wpar] = tad

        if not (parts & 64):
            k.barrier()
            k.flush(blk)
            s1.close()
            if stop <= 1:
                s12.close(); s13.close()
                return nc
        tsu = k.red(vec[:, 8:9], wsum[:, 0:32], ALU.add,
                    waits=(("s_act", k.cnt["s_act"]), vecfree[0]),
                    inc="s_dve")
        tsd = k.red(vec[:, 10:11], wsum[:, 32:64], ALU.add, inc="s_dve")
        m1 = k.mm(psc[0:1, 0:2], onescf[:], vec[:, 8:10], True, True,
                  waits=(tsu, tC["onescf"]), inc="s_pe")
        tc1 = k.ts("vector", sca[0:1, 0:1], psc[0:1, 0:1],
                   1.0 / float(IFF * D), float(EPS), ALU.mult, ALU.max,
                   waits=(m1, t_sz), inc="s_dve")
        m2 = k.mm(psc[0:1, 2:4], onescf[:], vec[:, 10:12], True, True,
                  waits=(tsd, tc1), inc="s_pe")
        tc2 = k.ts("vector", sca[0:1, 4:5], psc[0:1, 2:3],
                   1.0 / float(IFF * D), float(EPS), ALU.mult, ALU.max,
                   waits=(m2,), inc="s_dve")
        k.recip(sca[0:1, 2:3], sca[0:1, 0:1], waits=(tc1,))
        tr2 = k.recip(sca[0:1, 6:7], sca[0:1, 4:5], waits=(tc2,),
                      inc="s_dve")
        t_scal = None
        for i, (dstv, col) in enumerate(((supv, 0), (rsupv, 2), (sdnv, 4),
                                         (rsdnv, 6))):
            mb = k.mm(psc[:, 0:2], onesrf[:], sca[0:1, col:col + 2],
                      True, True, waits=(tr2, tC["onesrf"], tc2, t_scal),
                      inc="s_pe")
            t_scal = k.cp("vector", dstv[:], psc[:, 0:1], waits=(mb,),
                          inc="s_dve")

        tprev = vecfree[0]
        first = True
        for g in range(NKV):
            for hf in range(2):
                tsqk = k.act(dumpb[:], Kf[g][:, hf * D:(hf + 1) * D],
                             AF.Square, waits=(tok_kdone, tprev),
                             inc="s_act")
                for ch in range(4):
                    mm_ = k.mm(pkv[1][0:1, 0:512], onescb[:],
                               dumpb[:, ch * 512:(ch + 1) * 512],
                               True, True,
                               waits=[tsqk, tC["onescb"], tok_kv_free,
                                      tprev], inc="s_pe")
                    tred_ = k.red(sca[0:1, 11:12], pkv[1][0:1, 0:512],
                                  ALU.max, waits=(mm_,), inc="s_dve")
                    tok_kv_free = [tred_]
                    if first:
                        tprev = k.cp("vector", sca[0:1, 8:9],
                                     sca[0:1, 11:12], inc="s_dve")
                        first = False
                    else:
                        tprev = k.tt("vector", sca[0:1, 8:9],
                                     sca[0:1, 8:9], sca[0:1, 11:12],
                                     ALU.max, inc="s_dve")
        tmk2 = tprev
        k.barrier()
        k.flush(blk)
        if dbg and stop <= 1:
            k.dma("gpsimd", dbg_kv[:, 0:4096], Kf[0][:],
                  waits=[("s_pe", k.cnt["s_pe"]),
                         ("s_dve", k.cnt["s_dve"]),
                         ("s_act", k.cnt["s_act"]),
                         ("s_pool", k.cnt["s_pool"])])
            k.dma("gpsimd", dbg_kv[:, 4096:12288], Vt[0][:])
            k.dma("gpsimd", dbg_kv[:, 12288:12608], knsc[:])
            k.dma("gpsimd", dbg_kv[:, 12608:12736], cvn[:])
            k.dma("gpsimd", dbg_kv[:, 12736:14784], xfb[1][:])
            k.dma("gpsimd", dbg_kv[:, 14784:14912], ckvf[:])
            tpcp = k.cp("vector", scrB[:, 0:192], pdkv[n1 % 2 ^ 1][:],
                        inc="s_dve")
            tvcp = k.cp("vector", scrB[:, 192:224], vec[:], waits=(tpcp,),
                        inc="s_dve")
            k.dma("gpsimd", dbg_kv[:, 14912:15360],
                  scrB.bitcast(BF16)[:, 0:448], waits=(tvcp,))
            k.emit("gpsimd", lambda e: None,
                   waits=(("s_st", k.cnt["s_st"]),), chain=False)
            k.flush(blk)
        s1.close()
        if stop <= 1:
            s12.close(); s13.close()
            return nc

        s23 = contextlib.ExitStack()
        Q4 = [mk(s23, f"q4_{g}", (P, 4 * 512), BF16) for g in range(NKV)]
        E4 = [mk(s23, f"e4_{i}", (P, 512), BF16) for i in range(2)]
        outfb = [mk(s23, "outfb0", (P, H * VD), BF16)]
        masks = mk(s23, "masks", (P, 4 * 512), BF16)
        l32 = mk(s23, "l32", (1, 512), F32)
        reclb = mk(s23, "reclb", (1, 512), BF16)
        bcb = mk(s23, "bcb", (P, 512), BF16)
        scrA2 = mk(s23, "scrA2", (P, D), F32)
        scrB2 = mk(s23, "scrB2", (P, D), F32)
        rqc = mk(s23, "rqc", (P, 512), F32)
        rqs = mk(s23, "rqs", (P, 512), F32)
        gqbc = mk(s23, "gqbc", (P, CQ), F32)
        cqf = mk(s23, "cqf", (P, P), BF16)
        wqf = [mk(s23, "wqf0", (P, D), F32)]
        scrI = wqf[0].bitcast(I32)
        wqb = mk(s23, "wqb", (P, D), BF16)
        wqstage = mk(s23, "wqstage", (P, 16 * 256), BF16)

        t_masks = k.dma("gpsimd", masks[:], mask4)
        t_gq = k.dma("gpsimd", gqbc[:], gq_bc)

        pso = [s23.enter_context(nc.psum_tensor(f"pso{i}", [P, 512], F32))
               for i in range(4)]
        plq = s23.enter_context(nc.psum_tensor("plq", [1, 512], F32))
        pbc = s23.enter_context(nc.psum_tensor("pbc", [P, 512], F32))
        ptrw = s23.enter_context(nc.psum_tensor("ptrw", [P, 1024], BF16))

        bankfree = {i: [] for i in range(4)}
        tok_plq_free = []
        tok_pbc_free = []
        tok_ptrw_free = []
        tok_q4_free = [[] for _ in range(NKV)]
        tok_E_free = [None, None]
        tok_outf_free = [None, None]
        tok_wq_ld_free = [None, None]
        tok_stage_free = []
        tok_scrP_free = None
        tok_scrI_free = None
        tok_wqb_free = None
        tok_negb_rd = None
        tok_xq_free = [None, None]
        tok_xfq_free = [None, None]
        tok_cvn_free = None
        tok_cqf_free = None
        tok_rq_free = None

        nquant = [0]

        def quant_slot():
            for _ in range(2):
                j = nquant[0]
                if j >= 64:
                    return
                nquant[0] += 1
                nonlocal_tok = None
                up = j < 32
                jj = j if up else j - 32
                wpar = 0
                if up:
                    src = w_up[jj * P:(jj + 1) * P, :]
                    rsv = rsupv
                    it4 = jj % 2
                    grp = jj // 2
                else:
                    rr, cc_ = jj % 16, jj // 16
                    src = w_down[rr * P:(rr + 1) * P, cc_ * D:(cc_ + 1) * D]
                    rsv = rsdnv
                    it4 = rr % 2
                    grp = rr // 2
                tl = k.dma("scalar", wqf[wpar][:], src,
                           waits=(tok_wq_ld_free[wpar], t_scal),
                           sem="s_wq")
                ti_ = k.ts("vector", scrI[:], wqf[wpar][:], rsv[:], None,
                           ALU.mult, None, waits=(tl,), inc="s_dve")
                tc = k.ts("gpsimd", wqb[:], scrI[:], 1.0, -1.0, ALU.min,
                          ALU.max, waits=(ti_, tok_wqb_free), inc="s_pool")
                tok_wq_ld_free[wpar] = tc
                tok_scrI_free2 = tc
                tok_scrP_free2 = None
                trd = None
                for rnd in range(2):
                    trt = None
                    for bnum in range(8):
                        dc = rnd * 8 + bnum
                        trt = k.tr(ptrw[:, bnum * 128:(bnum + 1) * 128],
                                   wqb[:, dc * 128:(dc + 1) * 128], iden[:],
                                   waits=[tc] + tok_ptrw_free,
                                   inc="s_pe" if bnum == 7 else None)
                    dst = ap3(wqstage, it4 * 128 + rnd * 8 * 256,
                              [[16 * 256, P], [256, 8], [1, 128]])
                    trd = k.cp("vector", dst, ptrw[:],
                               waits=[trt] + tok_stage_free, inc="s_dve")
                    tok_ptrw_free.clear()
                    tok_ptrw_free.append(trd)
                tok_wqb_free2 = ("s_pe", k.cnt["s_pe"])
                nonlocal_tok = (tok_scrP_free2, tok_scrI_free2,
                                tok_wqb_free2)
                _set_quant_toks(nonlocal_tok)
                if it4 == 1:
                    base = grp * 256
                    if up:
                        dsl = ap3(wupT_d, base,
                                  [[IFF, P], [P * IFF, 16], [1, 256]])
                    else:
                        dsl = ap3(wdnT_d, cc_ * D * D + base,
                                  [[D, P], [P * D, 16], [1, 256]])
                    tout = k.dma("gpsimd", dsl, wqstage[:], waits=(trd,),
                                 sem="s_stg")
                    tok_stage_free.clear()
                    tok_stage_free.append(tout)

        def _set_quant_toks(toks):
            nonlocal tok_scrP_free, tok_scrI_free, tok_wqb_free
            tok_scrP_free, tok_scrI_free, tok_wqb_free = toks

        for quad in range(2):
            t_q4_last = None
            for kin in range(4):
                kq = quad * 4 + kin
                par = kin % 2
                tx = k.dma("gpsimd", xb[par][:],
                           x_own[kq * P:(kq + 1) * P, :],
                           waits=(tok_xq_free[par],), sem=f"s_x{par}")
                trc = k.dma("sync", rqc[:], ropeQc[kq * P:(kq + 1) * P, :],
                            waits=(tok_rq_free,))
                trs = k.dma("sync", rqs[:], ropeQs[kq * P:(kq + 1) * P, :])
                for half in range(2):
                    trt = None
                    for bnum in range(8):
                        dc = half * 8 + bnum
                        trt = k.tr(ptrw[:, bnum * 128:(bnum + 1) * 128],
                                   xb[par][:, dc * 128:(dc + 1) * 128],
                                   iden[:],
                                   waits=[tx] + tok_ptrw_free,
                                   inc="s_pe" if bnum == 7 else None)
                    tev = k.cp(
                        "vector", xfb[par][:, half * 1024:(half + 1) * 1024],
                        ptrw[:], waits=(trt, tok_xfq_free[par]),
                        inc="s_dve")
                    tok_ptrw_free.clear()
                    tok_ptrw_free.append(tev)
                tok_xq_free[par] = ("s_pe", k.cnt["s_pe"])
                txf = tok_ptrw_free[0]
                mmt = None
                for dc in range(16):
                    mmt = k.mm(pso[0][:, 0:128],
                               xfb[par][:, dc * 128:(dc + 1) * 128],
                               wdqb[:, dc * 128:(dc + 1) * 128],
                               start=(dc == 0), stop=(dc == 15),
                               waits=[txf, t_wdqb] + bankfree[0],
                               inc="s_pe" if dc == 15 else None)
                tok_xfq_free[par] = mmt
                tvar, tapp, tsub = ln_chain(pso[0][:, 0:128], CQ, None,
                                            scrA2[:, 0:CQ], mmt,
                                            extra=(tok_cvn_free,),
                                            app_extra=(k.tok.get("p2tgam"),))
                bankfree[0] = [tvar, tapp]
                tgam = k.tt("vector", cvn[:], scrA2[:, 0:CQ], gqbc[:],
                            ALU.mult, waits=(t_gq, tsub, tok_cvn_free),
                            inc="s_dve")
                k.tok["p2tgam"] = tgam
                ttq = k.tr(ptrw[:, 0:128], cvn[:], iden[:],
                           waits=[tgam] + tok_ptrw_free, inc="s_pe")
                tok_cvn_free = ttq
                tcqf = k.cp("vector", cqf[:], ptrw[:, 0:128],
                            waits=(ttq, tok_cqf_free), inc="s_dve")
                tok_ptrw_free.clear()
                tok_ptrw_free.append(tcqf)
                for c4 in range(4):
                    bk = 1 + (c4 % 2)
                    mmq = k.mm(pso[bk][:], cqf[:],
                               wuqb[:, c4 * 512:(c4 + 1) * 512], True, True,
                               waits=[tcqf, t_wuqb] + bankfree[bk],
                               inc="s_pe")
                    bankfree[bk] = [k.cp(
                        "vector", scrA2[:, c4 * 512:(c4 + 1) * 512],
                        pso[bk][:], waits=(mmq,), inc="s_dve")]
                tok_cqf_free = ("s_pe", k.cnt["s_pe"])
                tq32 = bankfree[2][0]
                xv = ap3(scrA2, 96, [[D, P], [128, 16], [1, 32]])
                xe = ap3(scrA2, 96, [[D, P], [128, 16], [2, 16]])
                xo = ap3(scrA2, 97, [[D, P], [128, 16], [2, 16]])
                swe = ap3(scrB2, 0, [[D, P], [32, 16], [2, 16]])
                swo = ap3(scrB2, 1, [[D, P], [32, 16], [2, 16]])
                swv = ap3(scrB2, 0, [[D, P], [32, 16], [1, 32]])
                k.cp("vector", swe, xo, waits=(tq32, bankfree[1][0]))
                tsw = k.cp("vector", swo, xe, inc="s_dve")
                k.tt("vector", scrB2[:, 512:1024], xv, rqc[:],
                     ALU.mult, waits=(trc,))
                k.tt("vector", swv, swv, rqs[:], ALU.mult, waits=(trs, tsw))
                trq = k.tt("vector", xv, scrB2[:, 512:1024], swv, ALU.add,
                           inc="s_dve")
                tok_rq_free = trq
                tqb = k.cp("vector", dumpb[:], scrA2[:],
                           waits=(trq, vecfree[0], k.tok.get("p2qtr")),
                           inc="s_dve")
                vecfree[0] = None
                for g in range(NKV):
                    trt = None
                    for hh in range(4):
                        h = g * 4 + hh
                        trt = k.tr(ptrw[:, hh * 128:(hh + 1) * 128],
                                   dumpb[:, h * 128:(h + 1) * 128], iden[:],
                                   waits=[tqb] + tok_ptrw_free
                                   + tok_q4_free[g],
                                   inc="s_pe" if hh == 3 else None)
                    tev = k.cp("vector",
                               Q4[g][:, kin * 512:(kin + 1) * 512],
                               ptrw[:, 0:512], waits=(trt,), inc="s_dve")
                    tok_ptrw_free.clear()
                    tok_ptrw_free.append(tev)
                    tok_q4_free[g] = []
                    t_q4_last = tev
                k.tok["p2qtr"] = ("s_pe", k.cnt["s_pe"])

            tprev = vecfree[0]
            firstq = True
            for g in range(NKV):
                tsqq = k.act(dumpb[:], Q4[g][:], AF.Square,
                             waits=(t_q4_last, tprev), inc="s_act")
                for ch in range(4):
                    mm_ = k.mm(pbc[0:1, 0:512], onescb[:],
                               dumpb[:, ch * 512:(ch + 1) * 512],
                               True, True,
                               waits=[tsqq, tC["onescb"], tprev]
                               + tok_pbc_free, inc="s_pe")
                    tred_ = k.red(sca[0:1, 11:12], pbc[0:1, 0:512],
                                  ALU.max, waits=(mm_,), inc="s_dve")
                    tok_pbc_free = [tred_]
                    if firstq:
                        tprev = k.cp("vector", sca[0:1, 9:10],
                                     sca[0:1, 11:12], inc="s_dve")
                        firstq = False
                    else:
                        tprev = k.tt("vector", sca[0:1, 9:10],
                                     sca[0:1, 9:10], sca[0:1, 11:12],
                                     ALU.max, inc="s_dve")
            vecfree[0] = tprev
            k.tt("vector", sca[0:1, 10:11], sca[0:1, 8:9], sca[0:1, 9:10],
                 ALU.mult, waits=(tmk2, tprev))
            tln_ = k.act(sca[0:1, 10:11], sca[0:1, 10:11], AF.Ln,
                         waits=(("s_dve", k.cnt["s_dve"]),), inc="s_act")
            texp_ = k.act(sca[0:1, 10:11], sca[0:1, 10:11], AF.Exp,
                          scale=0.5, inc="s_act")
            tnb = k.ts("vector", sca[0:1, 10:11], sca[0:1, 10:11],
                       -CSC, None, ALU.mult, None, waits=(texp_,),
                       inc="s_dve")
            mb = k.mm(pbc[:, 0:2], onesrf[:], sca[0:1, 10:12], True, True,
                      waits=[tnb, tC["onesrf"]] + tok_pbc_free, inc="s_pe")
            tnegb = k.cp("vector", negB[:], pbc[:, 0:1],
                         waits=(mb, tok_negb_rd), inc="s_dve")
            tok_pbc_free = [tnegb]

            for kin in range(4):
                kown = quad * 4 + kin
                nkt = 4 * (kown + 1)
                for g in range(NKV):
                    tpv = None
                    first = True
                    for kt in range(nkt):
                        spar = kt % 2
                        epar = kt % 2
                        tsm = k.mm(pso[spar][:],
                                   Kf[g][:, kt * P:(kt + 1) * P],
                                   Q4[g][:, kin * 512:(kin + 1) * 512],
                                   True, True,
                                   waits=[tnegb, tok_kdone, t_q4_last]
                                   + bankfree[spar],
                                   inc="s_pe")
                        texp = k.act(E4[epar][:], pso[spar][:], AF.Exp,
                                     scale=CSC, bias=negB[:],
                                     waits=(tsm, tok_E_free[epar]),
                                     inc="s_act")
                        bankfree[spar] = [texp]
                        tE = texp
                        j = kt - 4 * kown
                        if j >= 0:
                            tE = k.tt("vector", E4[epar][:], E4[epar][:],
                                      masks[:, j * 512:(j + 1) * 512],
                                      ALU.mult, waits=(texp, t_masks),
                                      inc="s_dve")
                        k.mm(plq[:], onescb[:], E4[epar][:],
                             first, kt == nkt - 1,
                             waits=[tE, tC["onescb"]]
                             + (tok_plq_free if first else []))
                        k.mm(pso[2][:], Vt[g][:, kt * VD:kt * VD + 128],
                             E4[epar][:], first, kt == nkt - 1,
                             waits=(bankfree[2] + bankfree[3])
                             if first else ())
                        tpv = k.mm(pso[3][:],
                                   Vt[g][:, kt * VD + 128:kt * VD + 256],
                                   E4[epar][:], first, kt == nkt - 1,
                                   inc="s_pe")
                        tok_E_free[epar] = tpv
                        first = False
                    tok_negb_rd = ("s_act", k.cnt["s_act"])
                    tlc = k.cp("vector", l32[:], plq[:], waits=(tpv,),
                               inc="s_dve")
                    tok_plq_free = [tlc]
                    k.recip(l32[:], l32[:])
                    trcl = k.cp("vector", reclb[:], l32[:], inc="s_dve")
                    mb2 = k.mm(pbc[:], onesrb[:], reclb[:], True, True,
                               waits=[trcl, tC["onesrb"]] + tok_pbc_free,
                               inc="s_pe")
                    tbc = k.cp("vector", bcb[:], pbc[:], waits=(mb2,),
                               inc="s_dve")
                    tok_pbc_free = [tbc]
                    opar = 0
                    dst0 = ap3(outfb[opar], g * 8 * 128,
                               [[H * VD, P], [256, 4], [1, 128]])
                    dst1 = ap3(outfb[opar], (g * 8 + 1) * 128,
                               [[H * VD, P], [256, 4], [1, 128]])
                    k.tt("vector", dst0, pso[2][:], bcb[:], ALU.mult,
                         waits=(tbc, tok_outf_free[opar]))
                    tev = k.tt("vector", dst1, pso[3][:], bcb[:], ALU.mult,
                               inc="s_dve")
                    bankfree[2] = [tev]
                    bankfree[3] = [tev]
                    tok_q4_free[g].append(tev)
                    if g == 3:
                        tok_outf_free[opar] = k.dma(
                            "gpsimd",
                            outf_dram[kown * P:(kown + 1) * P, :],
                            outfb[opar][:], waits=(tev,), sem="s_of")
                        k.tok[f"outf_dma{opar}"] = tok_outf_free[opar]
                    quant_slot()
        if dbg:
            k.dma("gpsimd", dbg_kv[:, 0:4096], Kf[0][:],
                  waits=[("s_pe", k.cnt["s_pe"]), ("s_dve", k.cnt["s_dve"])])
            k.dma("gpsimd", dbg_kv[:, 4096:4096 + 8192], Vt[0][:])
            k.dma("gpsimd", dbg_kv[:, 12288:14336], Q4[0][:])
            k.dma("gpsimd", dbg_kv[:, 14336:14848], E4[0][:])
            k.dma("gpsimd", dbg_kv[:, 14848:15360], bcb[:])
        k.barrier()
        k.flush(blk)
        s23.close()
        s12.close()
        s13.close()
        if stop <= 2:
            return nc

        s4 = contextlib.ExitStack()
        otb = [mk(s4, f"otb{i}", (P, H * VD), BF16) for i in range(4)]
        wob = [mk(s4, f"wob{i}", (P, 4 * D), BF16) for i in range(2)]
        xres = [mk(s4, f"xres{i}", (P, D), F32) for i in range(2)]
        yb = [mk(s4, f"yb{i}", (P, D), F32) for i in range(2)]
        scrA4 = mk(s4, "scrA4", (P, D), F32)
        pw = [s4.enter_context(nc.psum_tensor(f"pw{i}", [P, 512], F32))
              for i in range(8)]

        tok_ot_free = [None] * 4
        tok_wo_free = [None, None]
        tok_xr_free = [None, None]
        tok_y_free = [None, None]
        tok_pw_free = [[] for _ in range(8)]
        tok_x1_free = None

        for pair in range(4):
            ta, tb_ = pair * 2, pair * 2 + 1
            ia, ib = (pair % 2) * 2, (pair % 2) * 2 + 1
            tld_a = k.dma("scalar", otb[ia][:],
                          outf_dram[ta * P:(ta + 1) * P, :],
                          waits=(tok_ot_free[ia], k.tok.get("outf_dma0"),
                                 k.tok.get("outf_dma1")))
            tld_b = k.dma("scalar", otb[ib][:],
                          outf_dram[tb_ * P:(tb_ + 1) * P, :],
                          waits=(tok_ot_free[ib],))
            txa = k.dma("sync", xres[0][:], x_own[ta * P:(ta + 1) * P, :],
                        waits=(tok_xr_free[0],))
            txb = k.dma("sync", xres[1][:], x_own[tb_ * P:(tb_ + 1) * P, :],
                        waits=(tok_xr_free[1],))
            mml = None
            for hg in range(8):
                wpar = hg % 2
                wsrc = ap3(wo_bf_dram.tensor, hg * 4 * P * D,
                           [[D, P], [P * D, 4], [1, D]])
                twl = k.dma("sync", wob[wpar][:], wsrc,
                            waits=(tok_wo_free[wpar], t_wocast),
                            sem=f"s_wo{wpar}")
                for hh4 in range(4):
                    hv = hg * 4 + hh4
                    for ti, buf, tld in ((0, otb[ia], tld_a),
                                         (1, otb[ib], tld_b)):
                        for dc in range(4):
                            mml = k.mm(
                                pw[ti * 4 + dc][:],
                                buf[:, hv * 128:(hv + 1) * 128],
                                wob[wpar][:, hh4 * D + dc * 512:
                                           hh4 * D + (dc + 1) * 512],
                                hv == 0, hv == 31,
                                waits=[twl, tld]
                                + (tok_pw_free[ti * 4 + dc]
                                   if hv == 0 else []),
                                inc="s_pe" if (ti, dc) == (1, 3) else None)
                tok_wo_free[wpar] = ("s_pe", k.cnt["s_pe"])
            tok_ot_free[ia] = ("s_pe", k.cnt["s_pe"])
            tok_ot_free[ib] = tok_ot_free[ia]
            for ti in range(2):
                tglob = ta if ti == 0 else tb_
                txr = txa if ti == 0 else txb
                tlast = None
                for dc in range(4):
                    tlast = k.stt(
                        "vector", yb[ti][:, dc * 512:(dc + 1) * 512],
                        pw[ti * 4 + dc][:], 1.0,
                        xres[ti][:, dc * 512:(dc + 1) * 512],
                        ALU.mult, ALU.add,
                        waits=(mml, txr, tok_y_free[ti]), inc="s_dve")
                for dc in range(4):
                    tok_pw_free[ti * 4 + dc] = [tlast]
                tok_xr_free[ti] = tlast
                tvar, tapp, tx1 = ln_chain(yb[ti][:], D, None, scrA4[:],
                                           tlast, extra=(tok_x1_free,))
                tok_y_free[ti] = tapp
                tok_x1_free = k.dma("gpsimd",
                                    x1_dram[tglob * P:(tglob + 1) * P, :],
                                    scrA4[:], waits=(tx1,), sem="s_of")
        k.barrier()
        k.flush(blk)
        s4.close()
        if stop <= 3:
            return nc

        s5 = contextlib.ExitStack()
        hbuf = mk(s5, "hbuf", (P, OWN * IFF), BF16)
        scrA5 = mk(s5, "scrA5", (P, D), F32)
        scrI5 = scrA5.bitcast(I32)
        pup = [s5.enter_context(nc.psum_tensor(f"pup{i}", [P, 512], F32))
               for i in range(8)]
        s5a = contextlib.ExitStack()
        xqf = mk(s5a, "xqf", (P, OWN * 16 * 128), BF16)
        x1b = [mk(s5a, f"x1b{i}", (P, D), F32) for i in range(2)]
        wqs = [mk(s5a, f"wqs{i}", (P, 16 * 512), BF16) for i in range(2)]

        pupfree = {i: [] for i in range(8)}
        tok_x1b_free = [None, None]
        for t in range(OWN):
            par = t % 2
            tl = k.dma("scalar", x1b[par][:], x1_dram[t * P:(t + 1) * P, :],
                       waits=(tok_x1b_free[par], tok_x1_free))
            k.red(vec[:, 0:1], x1b[par][:], ALU.max, absval=True,
                  waits=(tl, vecfree[0]))
            k.ts("vector", vec[:, 0:1], vec[:, 0:1], float(EPS), None,
                 ALU.max, None)
            k.recip(vec[:, 1:2], vec[:, 0:1])
            tsc = k.ts("vector", vec[:, 1:2], vec[:, 1:2], 127.0, None,
                       ALU.mult, None, inc="s_dve")
            k.tt("vector", vec[:, 2:3], vec[:, 0:1], supv[:], ALU.mult,
                 waits=(t_scal,))
            tsr = k.ts("vector", srow1[:, t:t + 1], vec[:, 2:3],
                       1.0 / 127.0, None, ALU.mult, None, inc="s_dve")
            tq = k.act(scrA5[:], x1b[par][:], AF.Copy, scale=vec[:, 1:2],
                       waits=(tsc, k.tok.get("scrA5"),
                              k.tok.get("scrA5b")), inc="s_act")
            vecfree[0] = tq
            tok_x1b_free[par] = tq
            ti_ = k.cp("vector", scrI5[:], scrA5[:], waits=(tq,),
                       inc="s_dve")
            tcl = k.ts("vector", scrA5[:], scrI5[:], 127.0, -127.0, ALU.min,
                       ALU.max, waits=(ti_,), inc="s_dve")
            for pk in range(4):
                bk = pup[pk % 2]
                trt = None
                for bnum in range(4):
                    dc = pk * 4 + bnum
                    trt = k.tr(bk[:, bnum * 128:(bnum + 1) * 128],
                               scrA5[:, dc * 128:(dc + 1) * 128], idenf[:],
                               waits=[tcl] + pupfree[pk % 2],
                               inc="s_pe" if bnum == 3 else None)
                tev = k.cp(
                    "vector",
                    xqf[:, (t * 16 + pk * 4) * 128:
                        (t * 16 + pk * 4 + 4) * 128],
                    bk[:], waits=(trt,), inc="s_dve")
                pupfree[pk % 2] = [tev]
            k.tok["scrA5"] = ("s_pe", k.cnt["s_pe"])
        t_xqf = pupfree[1][0]

        wupT_r = wupT_d.rearrange("(dc p) i -> p dc i", p=P)
        tok_wqs_free = [None, None]
        t_quant_done = ("s_st", k.cnt["s_st"])
        t_h = None
        for ic in range(8):
            par = ic % 2
            tl = k.dma("sync", wqs[par][:],
                       wupT_r[:, :, ic * 512:(ic + 1) * 512],
                       waits=(tok_wqs_free[par], t_quant_done),
                       sem=f"s_wo{par}")
            mml = None
            for dc in range(16):
                for qs in range(8):
                    mml = k.mm(
                        pup[qs][:],
                        xqf[:, (qs * 16 + dc) * 128:
                            (qs * 16 + dc + 1) * 128],
                        wqs[par][:, dc * 512:(dc + 1) * 512],
                        dc == 0, dc == 15,
                        waits=[tl, t_xqf]
                        + (pupfree[qs] if dc == 0 else []),
                        inc="s_pe" if (dc, qs) == (15, 7) else None)
            tok_wqs_free[par] = ("s_pe", k.cnt["s_pe"])
            for qs in range(8):
                trl = k.act(scrA5[:, 0:512], pup[qs][:], AF.Relu,
                            scale=srow1[:, qs:qs + 1],
                            waits=(mml, tsr, k.tok.get("scrA5b")),
                            inc="s_act")
                pupfree[qs] = [trl]
                t_h = k.tt("vector",
                           hbuf[:, qs * IFF + ic * 512:
                                qs * IFF + (ic + 1) * 512],
                           scrA5[:, 0:512], scrA5[:, 0:512], ALU.mult,
                           waits=(trl,), inc="s_dve")
                k.tok["scrA5b"] = t_h

        t_hq = None
        for qs in range(8):
            k.red(vec[:, 0:1], hbuf[:, qs * IFF:(qs + 1) * IFF],
                  ALU.max, absval=True, waits=(t_h, vecfree[0]))
            k.ts("vector", vec[:, 0:1], vec[:, 0:1], float(EPS), None,
                 ALU.max, None)
            k.recip(vec[:, 1:2], vec[:, 0:1])
            tsc = k.ts("vector", vec[:, 1:2], vec[:, 1:2], 127.0, None,
                       ALU.mult, None, inc="s_dve")
            k.tt("vector", vec[:, 2:3], vec[:, 0:1], sdnv[:], ALU.mult)
            tsr2 = k.ts("vector", srow2[:, qs:qs + 1], vec[:, 2:3],
                        1.0 / 127.0, None, ALU.mult, None, inc="s_dve")
            for half in range(2):
                src = hbuf[:, qs * IFF + half * D:qs * IFF + (half + 1) * D]
                tq = k.act(scrA5[:], src, AF.Copy, scale=vec[:, 1:2],
                           waits=(tsc, k.tok.get("scrA5")), inc="s_act")
                vecfree[0] = tq
                ti_ = k.cp("vector", scrI5[:], scrA5[:], waits=(tq,),
                           inc="s_dve")
                tcl = k.ts("vector", scrA5[:], scrI5[:], 127.0, -127.0,
                           ALU.min, ALU.max, waits=(ti_,), inc="s_dve")
                for pk in range(4):
                    bk = pup[pk % 2]
                    trt = None
                    for bnum in range(4):
                        dcl = pk * 4 + bnum
                        trt = k.tr(bk[:, bnum * 128:(bnum + 1) * 128],
                                   scrA5[:, dcl * 128:(dcl + 1) * 128],
                                   idenf[:],
                                   waits=[tcl] + pupfree[pk % 2],
                                   inc="s_pe" if bnum == 3 else None)
                    dst = hbuf[:, qs * IFF + half * D + pk * 512:
                               qs * IFF + half * D + (pk + 1) * 512]
                    t_hq = k.cp("vector", dst, bk[:], waits=(trt, tq),
                                inc="s_dve")
                    pupfree[pk % 2] = [t_hq]
                k.tok["scrA5"] = ("s_pe", k.cnt["s_pe"])
        k.barrier()
        k.flush(blk)
        s5a.close()
        if stop <= 4:
            s5.close()
            return nc

        y2 = [mk(s5, f"y2_{i}", (P, D), F32) for i in range(4)]
        x1c = [mk(s5, f"x1c{i}", (P, 1024), F32) for i in range(2)]
        wdnb = [mk(s5, f"wdnb{i}", (P, 1024), BF16) for i in range(2)]
        pdn = pup
        pdnfree = {i: list(pupfree.get(i % 2, [])) for i in range(8)}
        tok_y2_free = [None] * 4
        for qq in range(2):
            for half in range(2):
                mml = None
                for ic in range(32):
                    par = ic % 2
                    tl = k.dma("sync", wdnb[par][:],
                               wdnT_d.ap()[ic * P:(ic + 1) * P,
                                           half * 1024:(half + 1) * 1024],
                               waits=(k.tok.get(f"wdn_free{par}"), t_hq),
                               sem=f"s_wo{par}")
                    for qs in range(4):
                        qsub = qq * 4 + qs
                        for hh in range(2):
                            mml = k.mm(
                                pdn[qs * 2 + hh][:],
                                hbuf[:, qsub * IFF + ic * 128:
                                     qsub * IFF + (ic + 1) * 128],
                                wdnb[par][:, hh * 512:(hh + 1) * 512],
                                ic == 0, ic == 31,
                                waits=[tl, t_hq]
                                + (pdnfree[qs * 2 + hh] if ic == 0 else []),
                                inc="s_pe" if (qs, hh) == (3, 1) else None)
                    k.tok[f"wdn_free{par}"] = ("s_pe", k.cnt["s_pe"])
                for qs in range(4):
                    qsub = qq * 4 + qs
                    par = qs % 2
                    tlx = k.dma("scalar", x1c[par][:],
                                x1_dram[qsub * P:(qsub + 1) * P,
                                        half * 1024:(half + 1) * 1024],
                                waits=(k.tok.get(f"x1c_free{par}"),))
                    tev = None
                    for hh in range(2):
                        tev = k.stt(
                            "vector",
                            y2[qs][:, half * 1024 + hh * 512:
                                   half * 1024 + (hh + 1) * 512],
                            pdn[qs * 2 + hh][:], srow2[:, qsub:qsub + 1],
                            x1c[par][:, hh * 512:(hh + 1) * 512],
                            ALU.mult, ALU.add,
                            waits=(mml, tlx, tsr2, tok_y2_free[qs]),
                            inc="s_dve")
                    pdnfree[qs * 2] = [tev]
                    pdnfree[qs * 2 + 1] = [tev]
                    k.tok[f"x1c_free{par}"] = tev
                    if half == 1:
                        k.tok[f"y2rdy{qs}"] = tev
            for qs in range(4):
                qsub = qq * 4 + qs
                tyev = k.tok[f"y2rdy{qs}"]
                tvar, tapp, tout = ln_chain(y2[qs][:], D, None, scrA5[:],
                                            tyev,
                                            extra=(k.tok.get("out_free"),
                                                   k.tok.get("scrA5"),))
                tok_y2_free[qs] = tapp
                k.tok["scrA5"] = None
                k.tok["out_free"] = k.dma(
                    "gpsimd", out_own[qsub * P:(qsub + 1) * P, :],
                    scrA5[:], waits=(tout,), sem="s_of")
        if dbg:
            def _dbgdma(dst, src):
                def f(e):
                    return e.dma_start(out=dst, in_=src,
                                       allow_slow_non_contiguous=True)
                return f
            k.emit("gpsimd", _dbgdma(dbg_sc[:, 0:1], negB.ap()),
                   waits=[("s_of", k.cnt["s_of"]),
                          ("s_dve", k.cnt["s_dve"]),
                          ("s_act", k.cnt["s_act"]),
                          ("s_st", k.cnt["s_st"])], inc="s_st", inc_by=16)
            k.emit("gpsimd", _dbgdma(dbg_sc[:, 1:33], vec.ap()),
                   waits=[("s_st", k.cnt["s_st"])], inc="s_st", inc_by=16)
            k.emit("gpsimd", _dbgdma(dbg_sc[0:1, 33:65], sca.ap()),
                   waits=[("s_st", k.cnt["s_st"])], inc="s_st", inc_by=16)
            k.dma("gpsimd", dbg_x1, x1_dram)
            k.dma("gpsimd", dbg_of, outf_dram)
            k.dma("gpsimd", dbg_wu, wupT_d.ap())
        k.emit("gpsimd", lambda e: None,
               waits=tuple((sn, k.cnt[sn]) for sn in
                           ("s_st", "s_of", "s_stg")), chain=False)
        k.flush(blk)
        s5.close()

    return nc


def _host_inputs(inputs, core):
    x = np.ascontiguousarray(inputs["L0_attn_x"], dtype=np.float32)
    s, cc = core // 4, core % 4
    xs = np.ascontiguousarray(x[s])
    xown = np.ascontiguousarray(
        xs.reshape(NT, P, D)[cc::4].reshape(OWN * P, D))

    w_dq = np.asarray(inputs["L0_attn_w_dq"], dtype=np.float32)
    w_dkv = np.asarray(inputs["L0_attn_w_dkv"], dtype=np.float32)
    w_uq = np.asarray(inputs["L0_attn_w_uq"], dtype=np.float32)
    wdq_t = np.ascontiguousarray(
        w_dq.reshape(16, P, CQ).transpose(1, 0, 2).reshape(P, 16 * CQ))
    wdkv_t = np.ascontiguousarray(
        w_dkv.reshape(16, P, CKV + KR).transpose(1, 0, 2)
        .reshape(P, 16 * (CKV + KR)))
    perm = np.concatenate(
        [np.concatenate([np.arange(96 * h, 96 * h + 96),
                         1536 + np.arange(32 * h, 32 * h + 32)])
         for h in range(H)])
    wuq_p = np.ascontiguousarray(w_uq[:, perm])

    gq = np.ascontiguousarray(np.broadcast_to(
        np.asarray(inputs["L0_attn_gamma_q"], np.float32)[None, :], (P, CQ)))
    gkv = np.ascontiguousarray(np.broadcast_to(
        np.asarray(inputs["L0_attn_gamma_kv"], np.float32)[None, :],
        (P, CKV)))

    def rope_cs(dd):
        inv = (1.0 / THETA ** (np.arange(0, dd, 2, dtype=np.float32) / dd)
               ).astype(np.float32)
        ang = (np.arange(S, dtype=np.float32)[:, None] * inv[None, :]
               ).astype(np.float32)
        return (np.cos(ang.astype(np.float64)).astype(np.float32),
                np.sin(ang.astype(np.float64)).astype(np.float32))

    def swapform(c, s_):
        n = c.shape[1]
        C2 = np.empty((S, 2 * n), np.float32)
        S2 = np.empty((S, 2 * n), np.float32)
        C2[:, 0::2] = c
        C2[:, 1::2] = c
        S2[:, 0::2] = -s_
        S2[:, 1::2] = s_
        return C2, S2

    ck, sk = rope_cs(KR)
    C2k, S2k = swapform(ck, sk)
    ropeKc = np.ascontiguousarray(
        C2k.reshape(NT, P, KR).transpose(1, 0, 2).reshape(P, NT * KR))
    ropeKs = np.ascontiguousarray(
        S2k.reshape(NT, P, KR).transpose(1, 0, 2).reshape(P, NT * KR))

    cq_, sq_ = rope_cs(QR)
    C2q, S2q = swapform(cq_, sq_)
    own_pos = (np.arange(NT)[cc::4][:, None] * P
               + np.arange(P)[None, :]).reshape(-1)
    C2qo = np.ascontiguousarray(np.tile(C2q[own_pos], (1, H)))
    S2qo = np.ascontiguousarray(np.tile(S2q[own_pos], (1, H)))

    tri = (np.arange(P)[:, None] <= np.arange(P)[None, :]).astype(np.float32)
    m = np.zeros((P, 4 * 512), np.float32)
    for j in range(4):
        pat = (np.ones((P, P), np.float32) if j < cc
               else (tri if j == cc else np.zeros((P, P), np.float32)))
        for hh in range(4):
            m[:, j * 512 + hh * P:j * 512 + (hh + 1) * P] = pat

    return {
        "x_seq": xs,
        "x_own": xown,
        "wdq_t": wdq_t,
        "wdkv_t": wdkv_t,
        "wuq_p": wuq_p,
        "wukv_i": np.ascontiguousarray(
            np.asarray(inputs["L0_attn_w_ukv"], np.float32)),
        "w_o": np.ascontiguousarray(
            np.asarray(inputs["L0_attn_w_o"], np.float32)),
        "w_up": np.ascontiguousarray(
            np.asarray(inputs["L0_ffn_w_up"], np.float32)),
        "w_down": np.ascontiguousarray(
            np.asarray(inputs["L0_ffn_w_down"], np.float32)),
        "gq_bc": gq,
        "gkv_bc": gkv,
        "ropeKc": ropeKc,
        "ropeKs": ropeKs,
        "ropeQc": C2qo,
        "ropeQs": S2qo,
        "mask4": m.astype(BFD),
        "iden_f": np.eye(P).astype(BFD),
        "iden_f32": np.eye(P).astype(np.float32),
        "ones_cb": np.ones((P, 1), BFD),
        "ones_rb": np.ones((1, P), BFD),
        "ones_cf": np.ones((P, 1), np.float32),
        "ones_rf": np.ones((1, P), np.float32),
    }


def kernel(**inputs):
    nc = build_program()
    in_maps = [_host_inputs(inputs, c) for c in range(8)]
    res = run_bass_kernel_spmd(nc, in_maps, core_ids=list(range(8)))
    out = np.empty((B, S, D), np.float32)
    for c in range(8):
        s, cc = c // 4, c % 4
        o = np.asarray(res.results[c]["out_own"]).reshape(OWN, P, D)
        out[s].reshape(NT, P, D)[cc::4] = o
    return out


# version 05
# speedup vs baseline: 1.2139x; 1.2139x over previous
import contextlib
import numpy as np
import ml_dtypes
import concourse.bass as bass
import concourse.mybir as mybir
from concourse.bass_utils import run_bass_kernel_spmd

F32 = mybir.dt.float32
BF16 = mybir.dt.bfloat16
I32 = mybir.dt.int32
AF = mybir.ActivationFunctionType
ALU = mybir.AluOpType
AX = mybir.AxisListType
BFD = ml_dtypes.bfloat16

B, S, D = 2, 4096, 2048
H, NKV = 16, 4
CQ, QN, QR = 128, 96, 32
CKV, KR = 128, 64
KN, VD = 64, 256
THETA = 1e4
EPS = 1e-5
CSC = float(1.0 / np.sqrt(128.0))
NEGB_MUL = float(-np.sqrt(128.0))
P = 128
NT = S // P
OWN = 8
IFF = 2 * D

ENGINES = ("sync", "tensor", "vector", "scalar", "gpsimd")


class Kern:
    ENGSEM = {"vector": "s_dve", "scalar": "s_act", "gpsimd": "s_pool"}

    def __init__(self, nc):
        self.nc = nc
        self.plan = {e: [] for e in ENGINES}
        self.hwm = {e: {} for e in ENGINES}
        self.cnt = {}
        self.sems = {}
        self.tok = {}
        self.prev = {}

    def add_sem(self, ctx, name):
        self.sems[name] = ctx.enter_context(self.nc.semaphore(name))
        self.cnt[name] = 0

    def emit(self, eng, fn, waits=(), inc=None, inc_by=1, chain=True):
        flat = []
        for w in waits:
            if w is None:
                continue
            if isinstance(w, list):
                flat.extend(x for x in w if x is not None)
            else:
                flat.append(w)
        if chain and eng in self.ENGSEM and inc_by == 1:
            es = self.ENGSEM[eng]
            if self.prev.get(eng) is not None:
                flat.append(self.prev[eng])
            if inc is None:
                inc = es
        t = None
        if inc is not None:
            self.cnt[inc] += inc_by
            t = (inc, self.cnt[inc])
        if chain and eng in self.ENGSEM and inc_by == 1:
            self.prev[eng] = t
        self.plan[eng].append((tuple(flat), fn, inc, inc_by, t))
        return t

    def flush(self, block):
        for eng in ENGINES:
            ops = self.plan[eng]
            if not ops:
                continue
            hwm = self.hwm[eng]
            sems = self.sems

            def body(e, ops=ops, hwm=hwm, sems=sems):
                for waits, fn, inc, inc_by, t in ops:
                    for (sn, val) in waits:
                        if hwm.get(sn, 0) < val:
                            e.wait_ge(sems[sn], val)
                            hwm[sn] = val
                    last = fn(e)
                    if inc is not None:
                        assert last is not None
                        last.then_inc(sems[inc], inc_by)

            getattr(block, eng)(body)
        self.plan = {e: [] for e in ENGINES}

    def dma(self, q, dst, src, waits=(), sem=None):
        sem = sem or {"sync": "s_ld", "scalar": "s_l2", "gpsimd": "s_st"}[q]
        prior = (sem, self.cnt[sem])
        return self.emit(q, lambda e: e.dma_start(out=dst, in_=src),
                         tuple(waits) + (prior,), inc=sem, inc_by=16)

    def mm(self, ps, lhsT, rhs, start, stop, waits=(), inc=None):
        return self.emit(
            "tensor",
            lambda e: e.matmul(ps, lhsT, rhs, start=start, stop=stop),
            waits, inc=inc)

    def tr(self, ps, src, iden, waits=(), inc=None):
        return self.emit("tensor", lambda e: e.transpose(ps, src, iden),
                         waits, inc=inc)

    def cp(self, eng, dst, src, waits=(), inc=None):
        return self.emit(eng, lambda e: e.tensor_copy(dst, src), waits,
                         inc=inc)

    def act(self, dst, src, func, waits=(), inc=None, scale=None, bias=None,
            accum_out=None):
        kw = {}
        if scale is not None:
            kw["scale"] = scale
        if bias is not None:
            kw["bias"] = bias
        if accum_out is not None:
            kw["accum_out"] = accum_out
        return self.emit("scalar",
                         lambda e: e.activation(dst, src, func, **kw),
                         waits, inc=inc)

    def ts(self, eng, dst, src, s1, s2, op0, op1, waits=(), inc=None):
        if op1 is None:
            return self.emit(
                eng,
                lambda e: e.tensor_scalar(dst, src, s1, None, op0=op0),
                waits, inc=inc)
        return self.emit(
            eng,
            lambda e: e.tensor_scalar(dst, src, s1, s2, op0=op0, op1=op1),
            waits, inc=inc)

    def tt(self, eng, dst, a, b, op, waits=(), inc=None):
        return self.emit(eng, lambda e: e.tensor_tensor(dst, a, b, op=op),
                         waits, inc=inc)

    def stt(self, eng, dst, a, sc, b, op0, op1, waits=(), inc=None):
        return self.emit(
            eng,
            lambda e: e.scalar_tensor_tensor(dst, a, sc, b, op0=op0, op1=op1),
            waits, inc=inc)

    def barrier(self):
        toks = [(sn, self.cnt[sn]) for sn in self.cnt]
        for eng in ENGINES:
            self.emit(eng, lambda e: None, waits=tuple(toks), chain=False)

    def recip(self, dst, src, waits=(), inc=None):
        return self.emit("vector", lambda e: e.reciprocal(dst, src),
                         waits, inc=inc)

    def red(self, dst, src, op, absval=False, waits=(), inc=None):
        if absval:
            return self.emit(
                "vector",
                lambda e: e.tensor_reduce(dst, src, axis=AX.X, op=op,
                                          apply_absolute_value=True),
                waits, inc=inc)
        return self.emit(
            "vector",
            lambda e: e.tensor_reduce(dst, src, axis=AX.X, op=op),
            waits, inc=inc)


def ap3(tensor, off, dims):
    return bass.AP(tensor, off, dims)


def build_program(stop=99, n1=NT, parts=65535, lnlvl=9, dbg=False):
    nc = bass.Bass("TRN2", target_bir_lowering=False,
                   detect_race_conditions=False)

    def din(name, shape, dtype=F32):
        return nc.dram_tensor(name, list(shape), dtype,
                              kind="ExternalInput").ap()

    x_seq = din("x_seq", (S, D))
    x_own = din("x_own", (OWN * P, D))
    wdq_t = din("wdq_t", (P, 16 * CQ))
    wdkv_t = din("wdkv_t", (P, 16 * (CKV + KR)))
    wuq_p = din("wuq_p", (CQ, 2048))
    wukv_i = din("wukv_i", (CKV, 1280))
    w_o = din("w_o", (H * VD, D))
    w_up = din("w_up", (IFF, D))
    w_down = din("w_down", (D, IFF))
    gq_bc = din("gq_bc", (P, CQ))
    gkv_bc = din("gkv_bc", (P, CKV))
    ropeKc = din("ropeKc", (P, NT * KR))
    ropeKs = din("ropeKs", (P, NT * KR))
    ropeQc = din("ropeQc", (OWN * P, H * QR))
    ropeQs = din("ropeQs", (OWN * P, H * QR))
    mask4 = din("mask4", (P, 4 * 512), BF16)
    iden_f = din("iden_f", (P, P), BF16)
    iden_f32 = din("iden_f32", (P, P))
    ones_cb = din("ones_cb", (P, 1), BF16)
    ones_rb = din("ones_rb", (1, P), BF16)
    ones_cf = din("ones_cf", (P, 1))
    ones_rf = din("ones_rf", (1, P))

    out_own = nc.dram_tensor("out_own", [OWN * P, D], F32,
                             kind="ExternalOutput").ap()
    dbg_x1 = dbg_of = dbg_wu = dbg_sc = None
    dbg_kv = None
    if dbg:
        dbg_sc = nc.dram_tensor("dbg_sc", [P, 66], F32,
                                kind="ExternalOutput").ap()
        dbg_kv = nc.dram_tensor("dbg_kv", [P, 3 * 8192], BF16,
                                kind="ExternalOutput").ap()
        dbg_x1 = nc.dram_tensor("dbg_x1", [OWN * P, D], F32,
                                kind="ExternalOutput").ap()
        dbg_of = nc.dram_tensor("dbg_of", [OWN * P, H * VD], BF16,
                                kind="ExternalOutput").ap()
        dbg_wu = nc.dram_tensor("dbg_wu", [D, IFF], BF16,
                                kind="ExternalOutput").ap()
    outf_dram = nc.dram_tensor("outf_dram", [OWN * P, H * VD], BF16,
                               kind="Internal").ap()
    x1_dram = nc.dram_tensor("x1_dram", [OWN * P, D], F32,
                             kind="Internal").ap()
    wo_bf_dram = nc.dram_tensor("wo_bf_dram", [H * VD, D], BF16,
                                kind="Internal").ap()
    wupT_d = nc.dram_tensor("wupT_d", [D, IFF], BF16, kind="Internal")
    wdnT_d = nc.dram_tensor("wdnT_d", [IFF, D], BF16, kind="Internal")

    with contextlib.ExitStack() as top:
        k = Kern(nc)
        for sname in ("s_ld", "s_l2", "s_st", "s_pe", "s_dve", "s_act",
                      "s_pool", "s_x0", "s_x1", "s_stg", "s_wq", "s_wo0",
                      "s_wo1", "s_of"):
            k.add_sem(top, sname)

        def mk(ctx, name, shape, dtype):
            return ctx.enter_context(nc.sbuf_tensor(name, list(shape), dtype))

        iden = mk(top, "iden", (P, P), BF16)
        idenf = mk(top, "idenf", (P, P), F32)
        onescb = mk(top, "onescb", (P, 1), BF16)
        onesrb = mk(top, "onesrb", (1, P), BF16)
        onescf = mk(top, "onescf", (P, 1), F32)
        onesrf = mk(top, "onesrf", (1, P), F32)
        vec = mk(top, "vec", (P, 32), F32)
        sca = mk(top, "sca", (1, 32), F32)
        wsum = mk(top, "wsum", (P, 64), F32)
        negB = mk(top, "negB", (P, 1), F32)
        supv = mk(top, "supv", (P, 1), F32)
        rsupv = mk(top, "rsupv", (P, 1), F32)
        sdnv = mk(top, "sdnv", (P, 1), F32)
        rsdnv = mk(top, "rsdnv", (P, 1), F32)
        srow1 = mk(top, "srow1", (P, OWN), F32)
        srow2 = mk(top, "srow2", (P, OWN), F32)
        dumpb = mk(top, "dumpb", (P, D), BF16)

        blk = top.enter_context(nc.Block())

        k.emit("gpsimd", lambda e: e.memset(vec[:], 0.0), inc="s_pool")
        t_vz = ("s_pool", k.cnt["s_pool"])
        k.emit("gpsimd", lambda e: e.memset(sca[:], 0.0), inc="s_pool")
        t_sz = ("s_pool", k.cnt["s_pool"])

        vecfree = [t_vz]

        def ln_chain(src_ps_or_sb, n, gamma_bc, out_dst, mmt, extra=(),
                     app_extra=()):
            if lnlvl == 16:
                tsq = k.act(dumpb[:, 0:min(n, D)], src_ps_or_sb, AF.Square,
                            waits=(mmt,), inc="s_act")
                vecfree[0] = tsq
                return tsq, tsq, tsq
            tred = k.red(vec[:, 0:1], src_ps_or_sb, ALU.add,
                         waits=(mmt, vecfree[0]) + tuple(extra),
                         inc="s_dve")
            if lnlvl < 1:
                vecfree[0] = tred
                return tred, tred, tred
            if lnlvl == 11:
                tsq = k.act(dumpb[:, 0:min(n, D)], src_ps_or_sb, AF.Square,
                            waits=(mmt, tred), inc="s_act")
                vecfree[0] = tsq
                return tred, tsq, tred
            if lnlvl == 12:
                tsq = k.act(dumpb[:, 0:min(n, D)], src_ps_or_sb, AF.Copy,
                            accum_out=vec[:, 1:2], waits=(mmt, tred),
                            inc="s_act")
                vecfree[0] = tsq
                return tred, tsq, tred
            tsq = k.act(dumpb[:, 0:min(n, D)], src_ps_or_sb, AF.Square,
                        accum_out=vec[:, 1:2], waits=(mmt, tred),
                        inc="s_act")
            if lnlvl < 2:
                vecfree[0] = tsq
                return tred, tsq, tred
            k.ts("vector", vec[:, 2:3], vec[:, 0:1], 1.0 / n, None,
                 ALU.mult, None)
            k.ts("vector", vec[:, 3:4], vec[:, 1:2], 1.0 / n, None,
                 ALU.mult, None, waits=(tsq,))
            k.tt("vector", vec[:, 4:5], vec[:, 2:3], vec[:, 2:3], ALU.mult)
            tvar = k.stt("vector", vec[:, 4:5], vec[:, 3:4], float(EPS),
                         vec[:, 4:5], ALU.add, ALU.subtract, inc="s_dve")
            if lnlvl < 3:
                vecfree[0] = tsq
                return tvar, tsq, tvar
            k.act(vec[:, 5:6], vec[:, 4:5], AF.Ln, waits=(tvar,))
            trst = k.act(vec[:, 5:6], vec[:, 5:6], AF.Exp, scale=-0.5,
                         inc="s_act")
            if lnlvl < 4:
                vecfree[0] = trst
                return tvar, trst, tvar
            tmr = k.tt("vector", vec[:, 6:7], vec[:, 2:3], vec[:, 5:6],
                       ALU.mult, waits=(trst,), inc="s_dve")
            if lnlvl < 5:
                vecfree[0] = trst
                return tvar, trst, tmr
            tapp = k.act(out_dst, src_ps_or_sb, AF.Copy, scale=vec[:, 5:6],
                         waits=(trst,) + tuple(app_extra), inc="s_act")
            tsub = k.ts("vector", out_dst, out_dst, vec[:, 6:7], None,
                        ALU.subtract, None, waits=(tapp, tmr), inc="s_dve")
            vecfree[0] = tapp
            return tvar, tapp, tsub

        s13 = contextlib.ExitStack()
        Kf = [mk(s13, f"kf{g}", (P, S), BF16) for g in range(NKV)]
        Vt = [mk(s13, f"vt{g}", (P, NT * VD), BF16) for g in range(NKV)]

        s12 = contextlib.ExitStack()
        wdqb = mk(s12, "wdqb", (P, 16 * CQ), BF16)
        wuqb = mk(s12, "wuqb", (CQ, 2048), BF16)
        xb = [mk(s12, f"xb{i}", (P, D), BF16) for i in range(2)]
        xfb = [mk(s12, f"xfb{i}", (P, D), BF16) for i in range(2)]
        cvn = mk(s12, "cvn", (P, P), BF16)

        tC = {}
        for nm, dst, src in [("iden", iden[:], iden_f),
                             ("idenf", idenf[:], iden_f32),
                             ("onescb", onescb[:], ones_cb),
                             ("onesrb", onesrb[:], ones_rb),
                             ("onescf", onescf[:], ones_cf),
                             ("onesrf", onesrf[:], ones_rf)]:
            tC[nm] = k.dma("gpsimd", dst, src)
        t_wdqb = k.dma("gpsimd", wdqb[:], wdq_t)
        t_wuqb = k.dma("gpsimd", wuqb[:], wuq_p)
        t_wocast = None
        if parts & 128:
            t_wocast = k.dma("gpsimd", wo_bf_dram, w_o)

        s1 = contextlib.ExitStack()
        wdkvb = mk(s1, "wdkvb", (P, 16 * (CKV + KR)), BF16)
        wukvb = mk(s1, "wukvb", (CKV, 1280), BF16)
        rkc = mk(s1, "rkc", (P, NT * KR), F32)
        rks = mk(s1, "rks", (P, NT * KR), F32)
        knsc = mk(s1, "knsc", (P, 320), BF16)
        scrA = mk(s1, "scrA1", (P, D), F32)
        scrB = mk(s1, "scrB1", (P, D), F32)
        gkvbc = mk(s1, "gkvbc", (P, CKV), F32)
        ckvf = mk(s1, "ckvf", (P, P), BF16)
        wlf = [mk(s1, f"wlf{i}", (P, D), F32) for i in range(2)]
        pdkv = [s1.enter_context(nc.psum_tensor(f"pdkv{i}", [P, 192], F32))
                for i in range(2)]
        pkv = [s1.enter_context(nc.psum_tensor("pkv0", [P, 512], F32)),
               s1.enter_context(nc.psum_tensor("pkv1", [P, 512], F32)),
               s1.enter_context(nc.psum_tensor("pkv2", [P, 256], F32))]
        ptr = [s1.enter_context(nc.psum_tensor(f"ptr{i}", [P, 1024], BF16))
               for i in range(2)]
        psc = s1.enter_context(nc.psum_tensor("psc", [P, 4], F32))

        if stop <= 0:
            k.flush(blk)
            s1.close(); s12.close(); s13.close()
            return nc
        t_wdkvb = k.dma("gpsimd", wdkvb[:], wdkv_t)
        t_wukvb = k.dma("gpsimd", wukvb[:], wukv_i)
        t_rkc = k.dma("gpsimd", rkc[:], ropeKc)
        t_rks = k.dma("gpsimd", rks[:], ropeKs)
        t_gkv = k.dma("gpsimd", gkvbc[:], gkv_bc)

        if not (parts & 1):
            k.emit("gpsimd", lambda e: e.memset(xfb[0][:], 0.01),
                   inc="s_pool")
            k.emit("gpsimd", lambda e: e.memset(xfb[1][:], 0.01),
                   inc="s_pool")
        t_xfmem = ("s_pool", k.cnt["s_pool"])
        tok_x_free = [None, None]
        tok_xfb_free = [None, None]
        tok_tt1 = None
        tok_tt4 = None
        tok_tgam_prev = None
        tok_dkv_free = [[], []]
        tok_kv_free = []
        tok_ptr_free = [None, None]
        tok_wl_free = [None, None]
        tok_kdone = None

        for t in range(n1):
            par = t % 2
            if not (parts & 1):
                txf = t_xfmem
                if not (parts & 2):
                    continue
                mmt = None
                for dc in range(16):
                    mmt = k.mm(pdkv[par][:],
                               xfb[par][:, dc * 128:(dc + 1) * 128],
                               wdkvb[:, dc * 192:(dc + 1) * 192],
                               start=(dc == 0), stop=(dc == 15),
                               waits=(txf, t_wdkvb, tok_dkv_free[par]),
                               inc="s_pe" if dc == 15 else None)
                tok_xfb_free[par] = mmt
            else:
                tx = k.dma("gpsimd", xb[par][:], x_seq[t * P:(t + 1) * P, :],
                           waits=(tok_x_free[par],), sem=f"s_x{par}")
                for half in range(2):
                    bank = ptr[1]
                    trt = None
                    for bnum in range(8):
                        dc = half * 8 + bnum
                        trt = k.tr(bank[:, bnum * 128:(bnum + 1) * 128],
                                   xb[par][:, dc * 128:(dc + 1) * 128],
                                   iden[:],
                                   waits=(tx, tC["iden"], tok_ptr_free[1]),
                                   inc="s_pe" if bnum == 7 else None)
                    tok_ptr_free[1] = k.cp(
                        "vector",
                        xfb[par][:, half * 1024:(half + 1) * 1024],
                        bank[:], waits=(trt, tok_xfb_free[par]),
                        inc="s_dve")
                tok_x_free[par] = ("s_pe", k.cnt["s_pe"])
                txf = tok_ptr_free[1]

                if not (parts & 2):
                    continue
                mmt = None
                for dc in range(16):
                    mmt = k.mm(pdkv[par][:],
                               xfb[par][:, dc * 128:(dc + 1) * 128],
                               wdkvb[:, dc * 192:(dc + 1) * 192],
                               start=(dc == 0), stop=(dc == 15),
                               waits=(txf, t_wdkvb, tok_dkv_free[par]),
                               inc="s_pe" if dc == 15 else None)
                tok_xfb_free[par] = mmt

            if not (parts & 4):
                continue
            ps = pdkv[par]
            tkr0 = k.cp("vector", scrA[:, 256:256 + KR],
                        ps[:, CKV:CKV + KR], waits=(mmt,))
            tvar, tapp, tsub = ln_chain(ps[:, 0:CKV], CKV, None,
                                        scrA[:, 0:CKV], mmt,
                                        app_extra=(tok_tgam_prev,))
            if lnlvl >= 11:
                tok_dkv_free[par] = [tapp, tvar]
                continue
            tgam = k.tt("vector", cvn[:], scrA[:, 0:CKV], gkvbc[:],
                        ALU.mult, waits=(t_gkv, tsub), inc="s_dve")

            if not (parts & 32):
                tok_dkv_free[par] = [tapp]
                continue
            k.cp("vector", ap3(scrA, 320, [[D, P], [2, KR // 2]]),
                 ap3(scrA, 257, [[D, P], [2, KR // 2]]))
            tsw = k.cp("vector", ap3(scrA, 321, [[D, P], [2, KR // 2]]),
                       ap3(scrA, 256, [[D, P], [2, KR // 2]]), inc="s_dve")
            k.tt("vector", scrB[:, 0:KR], scrA[:, 256:256 + KR],
                 rkc[:, t * KR:(t + 1) * KR], ALU.mult,
                 waits=(t_rkc, t_rks))
            k.tt("vector", scrB[:, 64:128], scrA[:, 320:320 + KR],
                 rks[:, t * KR:(t + 1) * KR], ALU.mult, waits=(tsw,))
            trr = k.tt("vector", knsc[:, 256:320], scrB[:, 0:64],
                       scrB[:, 64:128], ALU.add, waits=(tok_tt4,),
                       inc="s_dve")
            tok_dkv_free[par] = [trr, tapp]

            if not (parts & 8):
                continue
            tt1 = k.tr(ptr[0][:, 0:128], cvn[:], iden[:],
                       waits=(tgam, tok_ptr_free[0]), inc="s_pe")
            tok_tt1 = tt1
            tckvf = k.cp("vector", ckvf[:], ptr[0][:, 0:128], waits=(tt1,),
                         inc="s_dve")

            k.mm(pkv[0][:], ckvf[:], wukvb[:, 0:512], True, True,
                 waits=[tckvf, t_wukvb] + tok_kv_free)
            k.mm(pkv[1][:], ckvf[:], wukvb[:, 512:1024], True, True)
            tkv = k.mm(pkv[2][:], ckvf[:], wukvb[:, 1024:1280], True, True,
                       inc="s_pe")
            k.cp("vector", Vt[0][:, t * VD:(t + 1) * VD], pkv[0][:, 256:512],
                 waits=(tkv,))
            k.cp("vector", Vt[1][:, t * VD:(t + 1) * VD], pkv[1][:, 0:256])
            k.cp("vector", Vt[2][:, t * VD:(t + 1) * VD], pkv[1][:, 256:512])
            k.cp("vector", Vt[3][:, t * VD:(t + 1) * VD], pkv[2][:])
            tkn = k.cp("vector", knsc[:, 0:256], pkv[0][:, 0:256],
                       inc="s_dve")
            tok_kv_free = [tkn]
            k.tr(ptr[0][:, 128:256], knsc[:, 0:128], iden[:],
                 waits=(tkn, tckvf), inc="s_pe")
            k.tr(ptr[0][:, 256:384], knsc[:, 128:256], iden[:], inc="s_pe")
            tt4 = k.tr(ptr[0][0:64, 384:512], knsc[:, 256:320], iden[:],
                       waits=(trr,), inc="s_pe")
            tok_tt4 = tt4
            k.cp("vector", Kf[0][0:64, t * P:(t + 1) * P],
                 ptr[0][0:64, 128:256], waits=(tt4,))
            k.cp("vector", Kf[1][0:64, t * P:(t + 1) * P],
                 ptr[0][64:128, 128:256])
            k.cp("vector", Kf[2][0:64, t * P:(t + 1) * P],
                 ptr[0][0:64, 256:384])
            k.cp("vector", Kf[3][0:64, t * P:(t + 1) * P],
                 ptr[0][64:128, 256:384])
            for g in range(3):
                k.cp("vector", Kf[g][64:128, t * P:(t + 1) * P],
                     ptr[0][0:64, 384:512])
            tkall = k.cp("vector", Kf[3][64:128, t * P:(t + 1) * P],
                         ptr[0][0:64, 384:512], inc="s_dve")
            tok_ptr_free[0] = tkall
            tok_kdone = tkall

            if not (parts & 16):
                continue
            wpar = t % 2
            tu = k.dma("scalar", wlf[wpar][:], w_up[t * P:(t + 1) * P, :],
                       waits=(tok_wl_free[wpar],))
            tau = k.act(dumpb[:], wlf[wpar][:], AF.Abs,
                        accum_out=wsum[:, t:t + 1], waits=(tu,), inc="s_act")
            rr, cc_ = t % 16, t // 16
            td = k.dma("scalar", wlf[wpar][:],
                       w_down[rr * P:(rr + 1) * P, cc_ * D:(cc_ + 1) * D],
                       waits=(tau,))
            tad = k.act(dumpb[:], wlf[wpar][:], AF.Abs,
                        accum_out=wsum[:, 32 + t:33 + t], waits=(td,),
                        inc="s_act")
            tok_wl_free[wpar] = tad

        if not (parts & 64):
            k.barrier()
            k.flush(blk)
            s1.close()
            if stop <= 1:
                s12.close(); s13.close()
                return nc
        tsu = k.red(vec[:, 8:9], wsum[:, 0:32], ALU.add,
                    waits=(("s_act", k.cnt["s_act"]), vecfree[0]),
                    inc="s_dve")
        tsd = k.red(vec[:, 10:11], wsum[:, 32:64], ALU.add, inc="s_dve")
        m1 = k.mm(psc[0:1, 0:2], onescf[:], vec[:, 8:10], True, True,
                  waits=(tsu, tC["onescf"]), inc="s_pe")
        tc1 = k.ts("vector", sca[0:1, 0:1], psc[0:1, 0:1],
                   1.0 / float(IFF * D), float(EPS), ALU.mult, ALU.max,
                   waits=(m1, t_sz), inc="s_dve")
        m2 = k.mm(psc[0:1, 2:4], onescf[:], vec[:, 10:12], True, True,
                  waits=(tsd, tc1), inc="s_pe")
        tc2 = k.ts("vector", sca[0:1, 4:5], psc[0:1, 2:3],
                   1.0 / float(IFF * D), float(EPS), ALU.mult, ALU.max,
                   waits=(m2,), inc="s_dve")
        k.recip(sca[0:1, 2:3], sca[0:1, 0:1], waits=(tc1,))
        tr2 = k.recip(sca[0:1, 6:7], sca[0:1, 4:5], waits=(tc2,),
                      inc="s_dve")
        t_scal = None
        for i, (dstv, col) in enumerate(((supv, 0), (rsupv, 2), (sdnv, 4),
                                         (rsdnv, 6))):
            mb = k.mm(psc[:, 0:2], onesrf[:], sca[0:1, col:col + 2],
                      True, True, waits=(tr2, tC["onesrf"], tc2, t_scal),
                      inc="s_pe")
            t_scal = k.cp("vector", dstv[:], psc[:, 0:1], waits=(mb,),
                          inc="s_dve")

        tprev = vecfree[0]
        first = True
        for g in range(NKV):
            for hf in range(2):
                tsqk = k.act(dumpb[:], Kf[g][:, hf * D:(hf + 1) * D],
                             AF.Square, waits=(tok_kdone, tprev),
                             inc="s_act")
                for ch in range(4):
                    mm_ = k.mm(pkv[1][0:1, 0:512], onescb[:],
                               dumpb[:, ch * 512:(ch + 1) * 512],
                               True, True,
                               waits=[tsqk, tC["onescb"], tok_kv_free,
                                      tprev], inc="s_pe")
                    tred_ = k.red(sca[0:1, 11:12], pkv[1][0:1, 0:512],
                                  ALU.max, waits=(mm_,), inc="s_dve")
                    tok_kv_free = [tred_]
                    if first:
                        tprev = k.cp("vector", sca[0:1, 8:9],
                                     sca[0:1, 11:12], inc="s_dve")
                        first = False
                    else:
                        tprev = k.tt("vector", sca[0:1, 8:9],
                                     sca[0:1, 8:9], sca[0:1, 11:12],
                                     ALU.max, inc="s_dve")
        tmk2 = tprev
        k.barrier()
        k.flush(blk)
        if dbg and stop <= 1:
            k.dma("gpsimd", dbg_kv[:, 0:4096], Kf[0][:],
                  waits=[("s_pe", k.cnt["s_pe"]),
                         ("s_dve", k.cnt["s_dve"]),
                         ("s_act", k.cnt["s_act"]),
                         ("s_pool", k.cnt["s_pool"])])
            k.dma("gpsimd", dbg_kv[:, 4096:12288], Vt[0][:])
            k.dma("gpsimd", dbg_kv[:, 12288:12608], knsc[:])
            k.dma("gpsimd", dbg_kv[:, 12608:12736], cvn[:])
            k.dma("gpsimd", dbg_kv[:, 12736:14784], xfb[1][:])
            k.dma("gpsimd", dbg_kv[:, 14784:14912], ckvf[:])
            tpcp = k.cp("vector", scrB[:, 0:192], pdkv[n1 % 2 ^ 1][:],
                        inc="s_dve")
            tvcp = k.cp("vector", scrB[:, 192:224], vec[:], waits=(tpcp,),
                        inc="s_dve")
            k.dma("gpsimd", dbg_kv[:, 14912:15360],
                  scrB.bitcast(BF16)[:, 0:448], waits=(tvcp,))
            k.emit("gpsimd", lambda e: None,
                   waits=(("s_st", k.cnt["s_st"]),), chain=False)
            k.flush(blk)
        s1.close()
        if stop <= 1:
            s12.close(); s13.close()
            return nc

        s23 = contextlib.ExitStack()
        Q4 = [mk(s23, f"q4_{g}", (P, 4 * 512), BF16) for g in range(NKV)]
        E4 = [mk(s23, f"e4_{i}", (P, 512), BF16) for i in range(2)]
        outfb = [mk(s23, "outfb0", (P, H * VD), BF16)]
        masks = mk(s23, "masks", (P, 4 * 512), BF16)
        l32 = mk(s23, "l32", (1, 512), F32)
        reclb = mk(s23, "reclb", (1, 512), BF16)
        bcb = mk(s23, "bcb", (P, 512), BF16)
        scrA2 = mk(s23, "scrA2", (P, D), F32)
        scrB2 = mk(s23, "scrB2", (P, D), F32)
        rqc = mk(s23, "rqc", (P, 512), F32)
        rqs = mk(s23, "rqs", (P, 512), F32)
        gqbc = mk(s23, "gqbc", (P, CQ), F32)
        cqf = mk(s23, "cqf", (P, P), BF16)
        wqf = [mk(s23, "wqf0", (P, D), F32)]
        scrI = wqf[0].bitcast(I32)
        wqb = mk(s23, "wqb", (P, D), BF16)
        wqstage = mk(s23, "wqstage", (P, 16 * 256), BF16)

        t_masks = k.dma("gpsimd", masks[:], mask4)
        t_gq = k.dma("gpsimd", gqbc[:], gq_bc)

        pso = [s23.enter_context(nc.psum_tensor(f"pso{i}", [P, 512], F32))
               for i in range(4)]
        plq = s23.enter_context(nc.psum_tensor("plq", [1, 512], F32))
        pbc = s23.enter_context(nc.psum_tensor("pbc", [P, 512], F32))
        ptrw = s23.enter_context(nc.psum_tensor("ptrw", [P, 1024], BF16))

        bankfree = {i: [] for i in range(4)}
        tok_plq_free = []
        tok_pbc_free = []
        tok_ptrw_free = []
        tok_q4_free = [[] for _ in range(NKV)]
        tok_E_free = [None, None]
        tok_outf_free = [None, None]
        tok_wq_ld_free = [None, None]
        tok_stage_free = []
        tok_scrP_free = None
        tok_scrI_free = None
        tok_wqb_free = None
        tok_negb_rd = None
        tok_xq_free = [None, None]
        tok_xfq_free = [None, None]
        tok_cvn_free = None
        tok_cqf_free = None
        tok_rq_free = None

        nquant = [0]

        def quant_slot():
            for _ in range(2):
                j = nquant[0]
                if j >= 64:
                    return
                nquant[0] += 1
                nonlocal_tok = None
                up = j < 32
                jj = j if up else j - 32
                wpar = 0
                if up:
                    src = w_up[jj * P:(jj + 1) * P, :]
                    rsv = rsupv
                    it4 = jj % 2
                    grp = jj // 2
                else:
                    rr, cc_ = jj % 16, jj // 16
                    src = w_down[rr * P:(rr + 1) * P, cc_ * D:(cc_ + 1) * D]
                    rsv = rsdnv
                    it4 = rr % 2
                    grp = rr // 2
                tl = k.dma("scalar", wqf[wpar][:], src,
                           waits=(tok_wq_ld_free[wpar], t_scal),
                           sem="s_wq")
                ti_ = k.ts("vector", scrI[:], wqf[wpar][:], rsv[:], None,
                           ALU.mult, None, waits=(tl,), inc="s_dve")
                tc = k.ts("gpsimd", wqb[:], scrI[:], 1.0, -1.0, ALU.min,
                          ALU.max, waits=(ti_, tok_wqb_free), inc="s_pool")
                tok_wq_ld_free[wpar] = tc
                tok_scrI_free2 = tc
                tok_scrP_free2 = None
                trd = None
                for rnd in range(2):
                    trt = None
                    for bnum in range(8):
                        dc = rnd * 8 + bnum
                        trt = k.tr(ptrw[:, bnum * 128:(bnum + 1) * 128],
                                   wqb[:, dc * 128:(dc + 1) * 128], iden[:],
                                   waits=[tc] + tok_ptrw_free,
                                   inc="s_pe" if bnum == 7 else None)
                    dst = ap3(wqstage, it4 * 128 + rnd * 8 * 256,
                              [[16 * 256, P], [256, 8], [1, 128]])
                    trd = k.cp("vector", dst, ptrw[:],
                               waits=[trt] + tok_stage_free, inc="s_dve")
                    tok_ptrw_free.clear()
                    tok_ptrw_free.append(trd)
                tok_wqb_free2 = ("s_pe", k.cnt["s_pe"])
                nonlocal_tok = (tok_scrP_free2, tok_scrI_free2,
                                tok_wqb_free2)
                _set_quant_toks(nonlocal_tok)
                if it4 == 1:
                    base = grp * 256
                    if up:
                        dsl = ap3(wupT_d, base,
                                  [[IFF, P], [P * IFF, 16], [1, 256]])
                    else:
                        dsl = ap3(wdnT_d, cc_ * D * D + base,
                                  [[D, P], [P * D, 16], [1, 256]])
                    tout = k.dma("gpsimd", dsl, wqstage[:], waits=(trd,),
                                 sem="s_stg")
                    tok_stage_free.clear()
                    tok_stage_free.append(tout)

        def _set_quant_toks(toks):
            nonlocal tok_scrP_free, tok_scrI_free, tok_wqb_free
            tok_scrP_free, tok_scrI_free, tok_wqb_free = toks

        for quad in range(2):
            t_q4_last = None
            for kin in range(4):
                kq = quad * 4 + kin
                par = kin % 2
                tx = k.dma("gpsimd", xb[par][:],
                           x_own[kq * P:(kq + 1) * P, :],
                           waits=(tok_xq_free[par],), sem=f"s_x{par}")
                trc = k.dma("sync", rqc[:], ropeQc[kq * P:(kq + 1) * P, :],
                            waits=(tok_rq_free,))
                trs = k.dma("sync", rqs[:], ropeQs[kq * P:(kq + 1) * P, :])
                for half in range(2):
                    trt = None
                    for bnum in range(8):
                        dc = half * 8 + bnum
                        trt = k.tr(ptrw[:, bnum * 128:(bnum + 1) * 128],
                                   xb[par][:, dc * 128:(dc + 1) * 128],
                                   iden[:],
                                   waits=[tx] + tok_ptrw_free,
                                   inc="s_pe" if bnum == 7 else None)
                    tev = k.cp(
                        "vector", xfb[par][:, half * 1024:(half + 1) * 1024],
                        ptrw[:], waits=(trt, tok_xfq_free[par]),
                        inc="s_dve")
                    tok_ptrw_free.clear()
                    tok_ptrw_free.append(tev)
                tok_xq_free[par] = ("s_pe", k.cnt["s_pe"])
                txf = tok_ptrw_free[0]
                mmt = None
                for dc in range(16):
                    mmt = k.mm(pso[0][:, 0:128],
                               xfb[par][:, dc * 128:(dc + 1) * 128],
                               wdqb[:, dc * 128:(dc + 1) * 128],
                               start=(dc == 0), stop=(dc == 15),
                               waits=[txf, t_wdqb] + bankfree[0],
                               inc="s_pe" if dc == 15 else None)
                tok_xfq_free[par] = mmt
                tvar, tapp, tsub = ln_chain(pso[0][:, 0:128], CQ, None,
                                            scrA2[:, 0:CQ], mmt,
                                            extra=(tok_cvn_free,),
                                            app_extra=(k.tok.get("p2tgam"),))
                bankfree[0] = [tvar, tapp]
                tgam = k.tt("vector", cvn[:], scrA2[:, 0:CQ], gqbc[:],
                            ALU.mult, waits=(t_gq, tsub, tok_cvn_free),
                            inc="s_dve")
                k.tok["p2tgam"] = tgam
                ttq = k.tr(ptrw[:, 0:128], cvn[:], iden[:],
                           waits=[tgam] + tok_ptrw_free, inc="s_pe")
                tok_cvn_free = ttq
                tcqf = k.cp("vector", cqf[:], ptrw[:, 0:128],
                            waits=(ttq, tok_cqf_free), inc="s_dve")
                tok_ptrw_free.clear()
                tok_ptrw_free.append(tcqf)
                for c4 in range(4):
                    bk = 1 + (c4 % 2)
                    mmq = k.mm(pso[bk][:], cqf[:],
                               wuqb[:, c4 * 512:(c4 + 1) * 512], True, True,
                               waits=[tcqf, t_wuqb] + bankfree[bk],
                               inc="s_pe")
                    bankfree[bk] = [k.cp(
                        "vector", scrA2[:, c4 * 512:(c4 + 1) * 512],
                        pso[bk][:], waits=(mmq,), inc="s_dve")]
                tok_cqf_free = ("s_pe", k.cnt["s_pe"])
                tq32 = bankfree[2][0]
                xv = ap3(scrA2, 96, [[D, P], [128, 16], [1, 32]])
                xe = ap3(scrA2, 96, [[D, P], [128, 16], [2, 16]])
                xo = ap3(scrA2, 97, [[D, P], [128, 16], [2, 16]])
                swe = ap3(scrB2, 0, [[D, P], [32, 16], [2, 16]])
                swo = ap3(scrB2, 1, [[D, P], [32, 16], [2, 16]])
                swv = ap3(scrB2, 0, [[D, P], [32, 16], [1, 32]])
                k.cp("vector", swe, xo, waits=(tq32, bankfree[1][0]))
                tsw = k.cp("vector", swo, xe, inc="s_dve")
                k.tt("vector", scrB2[:, 512:1024], xv, rqc[:],
                     ALU.mult, waits=(trc,))
                k.tt("vector", swv, swv, rqs[:], ALU.mult, waits=(trs, tsw))
                trq = k.tt("vector", xv, scrB2[:, 512:1024], swv, ALU.add,
                           inc="s_dve")
                tok_rq_free = trq
                tqb = k.cp("vector", dumpb[:], scrA2[:],
                           waits=(trq, vecfree[0], k.tok.get("p2qtr")),
                           inc="s_dve")
                vecfree[0] = None
                for g in range(NKV):
                    trt = None
                    for hh in range(4):
                        h = g * 4 + hh
                        trt = k.tr(ptrw[:, hh * 128:(hh + 1) * 128],
                                   dumpb[:, h * 128:(h + 1) * 128], iden[:],
                                   waits=[tqb] + tok_ptrw_free
                                   + tok_q4_free[g],
                                   inc="s_pe" if hh == 3 else None)
                    tev = k.cp("vector",
                               Q4[g][:, kin * 512:(kin + 1) * 512],
                               ptrw[:, 0:512], waits=(trt,), inc="s_dve")
                    tok_ptrw_free.clear()
                    tok_ptrw_free.append(tev)
                    tok_q4_free[g] = []
                    t_q4_last = tev
                k.tok["p2qtr"] = ("s_pe", k.cnt["s_pe"])

            tprev = vecfree[0]
            firstq = True
            for g in range(NKV):
                tsqq = k.act(dumpb[:], Q4[g][:], AF.Square,
                             waits=(t_q4_last, tprev), inc="s_act")
                for ch in range(4):
                    mm_ = k.mm(pbc[0:1, 0:512], onescb[:],
                               dumpb[:, ch * 512:(ch + 1) * 512],
                               True, True,
                               waits=[tsqq, tC["onescb"], tprev]
                               + tok_pbc_free, inc="s_pe")
                    tred_ = k.red(sca[0:1, 11:12], pbc[0:1, 0:512],
                                  ALU.max, waits=(mm_,), inc="s_dve")
                    tok_pbc_free = [tred_]
                    if firstq:
                        tprev = k.cp("vector", sca[0:1, 9:10],
                                     sca[0:1, 11:12], inc="s_dve")
                        firstq = False
                    else:
                        tprev = k.tt("vector", sca[0:1, 9:10],
                                     sca[0:1, 9:10], sca[0:1, 11:12],
                                     ALU.max, inc="s_dve")
            vecfree[0] = tprev
            k.tt("vector", sca[0:1, 10:11], sca[0:1, 8:9], sca[0:1, 9:10],
                 ALU.mult, waits=(tmk2, tprev))
            tln_ = k.act(sca[0:1, 10:11], sca[0:1, 10:11], AF.Ln,
                         waits=(("s_dve", k.cnt["s_dve"]),), inc="s_act")
            texp_ = k.act(sca[0:1, 10:11], sca[0:1, 10:11], AF.Exp,
                          scale=0.5, inc="s_act")
            tnb = k.ts("vector", sca[0:1, 10:11], sca[0:1, 10:11],
                       -CSC, None, ALU.mult, None, waits=(texp_,),
                       inc="s_dve")
            mb = k.mm(pbc[:, 0:2], onesrf[:], sca[0:1, 10:12], True, True,
                      waits=[tnb, tC["onesrf"]] + tok_pbc_free, inc="s_pe")
            tnegb = k.cp("vector", negB[:], pbc[:, 0:1],
                         waits=(mb, tok_negb_rd), inc="s_dve")
            tok_pbc_free = [tnegb]

            for kin in range(4):
                kown = quad * 4 + kin
                nkt = 4 * (kown + 1)
                for g in range(NKV):
                    tpv = None
                    first = True

                    def emit_S(kt):
                        spar = kt % 2
                        return k.mm(pso[spar][:],
                                    Kf[g][:, kt * P:(kt + 1) * P],
                                    Q4[g][:, kin * 512:(kin + 1) * 512],
                                    True, True,
                                    waits=[tnegb, tok_kdone, t_q4_last]
                                    + bankfree[spar],
                                    inc="s_pe")

                    tsm_next = emit_S(0)
                    for kt in range(nkt):
                        spar = kt % 2
                        epar = kt % 2
                        tsm = tsm_next
                        texp = k.act(E4[epar][:], pso[spar][:], AF.Exp,
                                     scale=CSC, bias=negB[:],
                                     waits=(tsm, tok_E_free[epar]),
                                     inc="s_act")
                        bankfree[spar] = [texp]
                        if kt + 1 < nkt:
                            tsm_next = emit_S(kt + 1)
                        tE = texp
                        j = kt - 4 * kown
                        if j >= 0:
                            tE = k.tt("vector", E4[epar][:], E4[epar][:],
                                      masks[:, j * 512:(j + 1) * 512],
                                      ALU.mult, waits=(texp, t_masks),
                                      inc="s_dve")
                        k.mm(plq[:], onescb[:], E4[epar][:],
                             first, kt == nkt - 1,
                             waits=[tE, tC["onescb"]]
                             + (tok_plq_free if first else []))
                        k.mm(pso[2][:], Vt[g][:, kt * VD:kt * VD + 128],
                             E4[epar][:], first, kt == nkt - 1,
                             waits=(bankfree[2] + bankfree[3])
                             if first else ())
                        tpv = k.mm(pso[3][:],
                                   Vt[g][:, kt * VD + 128:kt * VD + 256],
                                   E4[epar][:], first, kt == nkt - 1,
                                   inc="s_pe")
                        tok_E_free[epar] = tpv
                        first = False
                    tok_negb_rd = ("s_act", k.cnt["s_act"])
                    tlc = k.cp("vector", l32[:], plq[:], waits=(tpv,),
                               inc="s_dve")
                    tok_plq_free = [tlc]
                    k.recip(l32[:], l32[:])
                    trcl = k.cp("vector", reclb[:], l32[:], inc="s_dve")
                    mb2 = k.mm(pbc[:], onesrb[:], reclb[:], True, True,
                               waits=[trcl, tC["onesrb"]] + tok_pbc_free,
                               inc="s_pe")
                    tbc = k.cp("vector", bcb[:], pbc[:], waits=(mb2,),
                               inc="s_dve")
                    tok_pbc_free = [tbc]
                    opar = 0
                    dst0 = ap3(outfb[opar], g * 8 * 128,
                               [[H * VD, P], [256, 4], [1, 128]])
                    dst1 = ap3(outfb[opar], (g * 8 + 1) * 128,
                               [[H * VD, P], [256, 4], [1, 128]])
                    k.tt("vector", dst0, pso[2][:], bcb[:], ALU.mult,
                         waits=(tbc, tok_outf_free[opar]))
                    tev = k.tt("vector", dst1, pso[3][:], bcb[:], ALU.mult,
                               inc="s_dve")
                    bankfree[2] = [tev]
                    bankfree[3] = [tev]
                    tok_q4_free[g].append(tev)
                    if g == 3:
                        tok_outf_free[opar] = k.dma(
                            "gpsimd",
                            outf_dram[kown * P:(kown + 1) * P, :],
                            outfb[opar][:], waits=(tev,), sem="s_of")
                        k.tok[f"outf_dma{opar}"] = tok_outf_free[opar]
                    quant_slot()
        if dbg:
            k.dma("gpsimd", dbg_kv[:, 0:4096], Kf[0][:],
                  waits=[("s_pe", k.cnt["s_pe"]), ("s_dve", k.cnt["s_dve"])])
            k.dma("gpsimd", dbg_kv[:, 4096:4096 + 8192], Vt[0][:])
            k.dma("gpsimd", dbg_kv[:, 12288:14336], Q4[0][:])
            k.dma("gpsimd", dbg_kv[:, 14336:14848], E4[0][:])
            k.dma("gpsimd", dbg_kv[:, 14848:15360], bcb[:])
        k.barrier()
        k.flush(blk)
        s23.close()
        s12.close()
        s13.close()
        if stop <= 2:
            return nc

        s4 = contextlib.ExitStack()
        otb = [mk(s4, f"otb{i}", (P, H * VD), BF16) for i in range(4)]
        wob = [mk(s4, f"wob{i}", (P, 4 * D), BF16) for i in range(2)]
        xres = [mk(s4, f"xres{i}", (P, D), F32) for i in range(2)]
        yb = [mk(s4, f"yb{i}", (P, D), F32) for i in range(2)]
        scrA4 = mk(s4, "scrA4", (P, D), F32)
        pw = [s4.enter_context(nc.psum_tensor(f"pw{i}", [P, 512], F32))
              for i in range(8)]

        tok_ot_free = [None] * 4
        tok_wo_free = [None, None]
        tok_xr_free = [None, None]
        tok_y_free = [None, None]
        tok_pw_free = [[] for _ in range(8)]
        tok_x1_free = None

        for pair in range(4):
            ta, tb_ = pair * 2, pair * 2 + 1
            ia, ib = (pair % 2) * 2, (pair % 2) * 2 + 1
            tld_a = k.dma("scalar", otb[ia][:],
                          outf_dram[ta * P:(ta + 1) * P, :],
                          waits=(tok_ot_free[ia], k.tok.get("outf_dma0"),
                                 k.tok.get("outf_dma1")))
            tld_b = k.dma("scalar", otb[ib][:],
                          outf_dram[tb_ * P:(tb_ + 1) * P, :],
                          waits=(tok_ot_free[ib],))
            txa = k.dma("sync", xres[0][:], x_own[ta * P:(ta + 1) * P, :],
                        waits=(tok_xr_free[0],))
            txb = k.dma("sync", xres[1][:], x_own[tb_ * P:(tb_ + 1) * P, :],
                        waits=(tok_xr_free[1],))
            mml = None
            for hg in range(8):
                wpar = hg % 2
                wsrc = ap3(wo_bf_dram.tensor, hg * 4 * P * D,
                           [[D, P], [P * D, 4], [1, D]])
                twl = k.dma("sync", wob[wpar][:], wsrc,
                            waits=(tok_wo_free[wpar], t_wocast),
                            sem=f"s_wo{wpar}")
                for hh4 in range(4):
                    hv = hg * 4 + hh4
                    for ti, buf, tld in ((0, otb[ia], tld_a),
                                         (1, otb[ib], tld_b)):
                        for dc in range(4):
                            mml = k.mm(
                                pw[ti * 4 + dc][:],
                                buf[:, hv * 128:(hv + 1) * 128],
                                wob[wpar][:, hh4 * D + dc * 512:
                                           hh4 * D + (dc + 1) * 512],
                                hv == 0, hv == 31,
                                waits=[twl, tld]
                                + (tok_pw_free[ti * 4 + dc]
                                   if hv == 0 else []),
                                inc="s_pe" if (ti, dc) == (1, 3) else None)
                tok_wo_free[wpar] = ("s_pe", k.cnt["s_pe"])
            tok_ot_free[ia] = ("s_pe", k.cnt["s_pe"])
            tok_ot_free[ib] = tok_ot_free[ia]
            for ti in range(2):
                tglob = ta if ti == 0 else tb_
                txr = txa if ti == 0 else txb
                tlast = None
                for dc in range(4):
                    tlast = k.stt(
                        "vector", yb[ti][:, dc * 512:(dc + 1) * 512],
                        pw[ti * 4 + dc][:], 1.0,
                        xres[ti][:, dc * 512:(dc + 1) * 512],
                        ALU.mult, ALU.add,
                        waits=(mml, txr, tok_y_free[ti]), inc="s_dve")
                for dc in range(4):
                    tok_pw_free[ti * 4 + dc] = [tlast]
                tok_xr_free[ti] = tlast
                tvar, tapp, tx1 = ln_chain(yb[ti][:], D, None, scrA4[:],
                                           tlast, extra=(tok_x1_free,))
                tok_y_free[ti] = tapp
                tok_x1_free = k.dma("gpsimd",
                                    x1_dram[tglob * P:(tglob + 1) * P, :],
                                    scrA4[:], waits=(tx1,), sem="s_of")
        k.barrier()
        k.flush(blk)
        s4.close()
        if stop <= 3:
            return nc

        s5 = contextlib.ExitStack()
        hbuf = mk(s5, "hbuf", (P, OWN * IFF), BF16)
        scrA5 = mk(s5, "scrA5", (P, D), F32)
        scrI5 = scrA5.bitcast(I32)
        pup = [s5.enter_context(nc.psum_tensor(f"pup{i}", [P, 512], F32))
               for i in range(8)]
        s5a = contextlib.ExitStack()
        xqf = mk(s5a, "xqf", (P, OWN * 16 * 128), BF16)
        x1b = [mk(s5a, f"x1b{i}", (P, D), F32) for i in range(2)]
        wqs = [mk(s5a, f"wqs{i}", (P, 16 * 512), BF16) for i in range(2)]

        pupfree = {i: [] for i in range(8)}
        tok_x1b_free = [None, None]
        for t in range(OWN):
            par = t % 2
            tl = k.dma("scalar", x1b[par][:], x1_dram[t * P:(t + 1) * P, :],
                       waits=(tok_x1b_free[par], tok_x1_free))
            k.red(vec[:, 0:1], x1b[par][:], ALU.max, absval=True,
                  waits=(tl, vecfree[0]))
            k.ts("vector", vec[:, 0:1], vec[:, 0:1], float(EPS), None,
                 ALU.max, None)
            k.recip(vec[:, 1:2], vec[:, 0:1])
            tsc = k.ts("vector", vec[:, 1:2], vec[:, 1:2], 127.0, None,
                       ALU.mult, None, inc="s_dve")
            k.tt("vector", vec[:, 2:3], vec[:, 0:1], supv[:], ALU.mult,
                 waits=(t_scal,))
            tsr = k.ts("vector", srow1[:, t:t + 1], vec[:, 2:3],
                       1.0 / 127.0, None, ALU.mult, None, inc="s_dve")
            tq = k.act(scrA5[:], x1b[par][:], AF.Copy, scale=vec[:, 1:2],
                       waits=(tsc, k.tok.get("scrA5"),
                              k.tok.get("scrA5b")), inc="s_act")
            vecfree[0] = tq
            tok_x1b_free[par] = tq
            ti_ = k.cp("vector", scrI5[:], scrA5[:], waits=(tq,),
                       inc="s_dve")
            tcl = k.ts("vector", scrA5[:], scrI5[:], 127.0, -127.0, ALU.min,
                       ALU.max, waits=(ti_,), inc="s_dve")
            for pk in range(4):
                bk = pup[pk % 2]
                trt = None
                for bnum in range(4):
                    dc = pk * 4 + bnum
                    trt = k.tr(bk[:, bnum * 128:(bnum + 1) * 128],
                               scrA5[:, dc * 128:(dc + 1) * 128], idenf[:],
                               waits=[tcl] + pupfree[pk % 2],
                               inc="s_pe" if bnum == 3 else None)
                tev = k.cp(
                    "vector",
                    xqf[:, (t * 16 + pk * 4) * 128:
                        (t * 16 + pk * 4 + 4) * 128],
                    bk[:], waits=(trt,), inc="s_dve")
                pupfree[pk % 2] = [tev]
            k.tok["scrA5"] = ("s_pe", k.cnt["s_pe"])
        t_xqf = pupfree[1][0]

        wupT_r = wupT_d.rearrange("(dc p) i -> p dc i", p=P)
        tok_wqs_free = [None, None]
        t_quant_done = ("s_st", k.cnt["s_st"])
        t_h = None
        for ic in range(8):
            par = ic % 2
            tl = k.dma("sync", wqs[par][:],
                       wupT_r[:, :, ic * 512:(ic + 1) * 512],
                       waits=(tok_wqs_free[par], t_quant_done),
                       sem=f"s_wo{par}")
            mml = None
            for dc in range(16):
                for qs in range(8):
                    mml = k.mm(
                        pup[qs][:],
                        xqf[:, (qs * 16 + dc) * 128:
                            (qs * 16 + dc + 1) * 128],
                        wqs[par][:, dc * 512:(dc + 1) * 512],
                        dc == 0, dc == 15,
                        waits=[tl, t_xqf]
                        + (pupfree[qs] if dc == 0 else []),
                        inc="s_pe" if (dc, qs) == (15, 7) else None)
            tok_wqs_free[par] = ("s_pe", k.cnt["s_pe"])
            for qs in range(8):
                trl = k.act(scrA5[:, 0:512], pup[qs][:], AF.Relu,
                            scale=srow1[:, qs:qs + 1],
                            waits=(mml, tsr, k.tok.get("scrA5b")),
                            inc="s_act")
                pupfree[qs] = [trl]
                t_h = k.tt("vector",
                           hbuf[:, qs * IFF + ic * 512:
                                qs * IFF + (ic + 1) * 512],
                           scrA5[:, 0:512], scrA5[:, 0:512], ALU.mult,
                           waits=(trl,), inc="s_dve")
                k.tok["scrA5b"] = t_h

        t_hq = None
        for qs in range(8):
            k.red(vec[:, 0:1], hbuf[:, qs * IFF:(qs + 1) * IFF],
                  ALU.max, absval=True, waits=(t_h, vecfree[0]))
            k.ts("vector", vec[:, 0:1], vec[:, 0:1], float(EPS), None,
                 ALU.max, None)
            k.recip(vec[:, 1:2], vec[:, 0:1])
            tsc = k.ts("vector", vec[:, 1:2], vec[:, 1:2], 127.0, None,
                       ALU.mult, None, inc="s_dve")
            k.tt("vector", vec[:, 2:3], vec[:, 0:1], sdnv[:], ALU.mult)
            tsr2 = k.ts("vector", srow2[:, qs:qs + 1], vec[:, 2:3],
                        1.0 / 127.0, None, ALU.mult, None, inc="s_dve")
            for half in range(2):
                src = hbuf[:, qs * IFF + half * D:qs * IFF + (half + 1) * D]
                tq = k.act(scrA5[:], src, AF.Copy, scale=vec[:, 1:2],
                           waits=(tsc, k.tok.get("scrA5")), inc="s_act")
                vecfree[0] = tq
                ti_ = k.cp("vector", scrI5[:], scrA5[:], waits=(tq,),
                           inc="s_dve")
                tcl = k.ts("vector", scrA5[:], scrI5[:], 127.0, -127.0,
                           ALU.min, ALU.max, waits=(ti_,), inc="s_dve")
                for pk in range(4):
                    bk = pup[pk % 2]
                    trt = None
                    for bnum in range(4):
                        dcl = pk * 4 + bnum
                        trt = k.tr(bk[:, bnum * 128:(bnum + 1) * 128],
                                   scrA5[:, dcl * 128:(dcl + 1) * 128],
                                   idenf[:],
                                   waits=[tcl] + pupfree[pk % 2],
                                   inc="s_pe" if bnum == 3 else None)
                    dst = hbuf[:, qs * IFF + half * D + pk * 512:
                               qs * IFF + half * D + (pk + 1) * 512]
                    t_hq = k.cp("vector", dst, bk[:], waits=(trt, tq),
                                inc="s_dve")
                    pupfree[pk % 2] = [t_hq]
                k.tok["scrA5"] = ("s_pe", k.cnt["s_pe"])
        k.barrier()
        k.flush(blk)
        s5a.close()
        if stop <= 4:
            s5.close()
            return nc

        y2 = [mk(s5, f"y2_{i}", (P, D), F32) for i in range(4)]
        x1c = [mk(s5, f"x1c{i}", (P, 1024), F32) for i in range(2)]
        wdnb = [mk(s5, f"wdnb{i}", (P, 1024), BF16) for i in range(2)]
        pdn = pup
        pdnfree = {i: list(pupfree.get(i % 2, [])) for i in range(8)}
        tok_y2_free = [None] * 4
        for qq in range(2):
            for half in range(2):
                mml = None
                for ic in range(32):
                    par = ic % 2
                    tl = k.dma("sync", wdnb[par][:],
                               wdnT_d.ap()[ic * P:(ic + 1) * P,
                                           half * 1024:(half + 1) * 1024],
                               waits=(k.tok.get(f"wdn_free{par}"), t_hq),
                               sem=f"s_wo{par}")
                    for qs in range(4):
                        qsub = qq * 4 + qs
                        for hh in range(2):
                            mml = k.mm(
                                pdn[qs * 2 + hh][:],
                                hbuf[:, qsub * IFF + ic * 128:
                                     qsub * IFF + (ic + 1) * 128],
                                wdnb[par][:, hh * 512:(hh + 1) * 512],
                                ic == 0, ic == 31,
                                waits=[tl, t_hq]
                                + (pdnfree[qs * 2 + hh] if ic == 0 else []),
                                inc="s_pe" if (qs, hh) == (3, 1) else None)
                    k.tok[f"wdn_free{par}"] = ("s_pe", k.cnt["s_pe"])
                for qs in range(4):
                    qsub = qq * 4 + qs
                    par = qs % 2
                    tlx = k.dma("scalar", x1c[par][:],
                                x1_dram[qsub * P:(qsub + 1) * P,
                                        half * 1024:(half + 1) * 1024],
                                waits=(k.tok.get(f"x1c_free{par}"),))
                    tev = None
                    for hh in range(2):
                        tev = k.stt(
                            "vector",
                            y2[qs][:, half * 1024 + hh * 512:
                                   half * 1024 + (hh + 1) * 512],
                            pdn[qs * 2 + hh][:], srow2[:, qsub:qsub + 1],
                            x1c[par][:, hh * 512:(hh + 1) * 512],
                            ALU.mult, ALU.add,
                            waits=(mml, tlx, tsr2, tok_y2_free[qs]),
                            inc="s_dve")
                    pdnfree[qs * 2] = [tev]
                    pdnfree[qs * 2 + 1] = [tev]
                    k.tok[f"x1c_free{par}"] = tev
                    if half == 1:
                        k.tok[f"y2rdy{qs}"] = tev
            for qs in range(4):
                qsub = qq * 4 + qs
                tyev = k.tok[f"y2rdy{qs}"]
                tvar, tapp, tout = ln_chain(y2[qs][:], D, None, scrA5[:],
                                            tyev,
                                            extra=(k.tok.get("out_free"),
                                                   k.tok.get("scrA5"),))
                tok_y2_free[qs] = tapp
                k.tok["scrA5"] = None
                k.tok["out_free"] = k.dma(
                    "gpsimd", out_own[qsub * P:(qsub + 1) * P, :],
                    scrA5[:], waits=(tout,), sem="s_of")
        if dbg:
            def _dbgdma(dst, src):
                def f(e):
                    return e.dma_start(out=dst, in_=src,
                                       allow_slow_non_contiguous=True)
                return f
            k.emit("gpsimd", _dbgdma(dbg_sc[:, 0:1], negB.ap()),
                   waits=[("s_of", k.cnt["s_of"]),
                          ("s_dve", k.cnt["s_dve"]),
                          ("s_act", k.cnt["s_act"]),
                          ("s_st", k.cnt["s_st"])], inc="s_st", inc_by=16)
            k.emit("gpsimd", _dbgdma(dbg_sc[:, 1:33], vec.ap()),
                   waits=[("s_st", k.cnt["s_st"])], inc="s_st", inc_by=16)
            k.emit("gpsimd", _dbgdma(dbg_sc[0:1, 33:65], sca.ap()),
                   waits=[("s_st", k.cnt["s_st"])], inc="s_st", inc_by=16)
            k.dma("gpsimd", dbg_x1, x1_dram)
            k.dma("gpsimd", dbg_of, outf_dram)
            k.dma("gpsimd", dbg_wu, wupT_d.ap())
        k.emit("gpsimd", lambda e: None,
               waits=tuple((sn, k.cnt[sn]) for sn in
                           ("s_st", "s_of", "s_stg")), chain=False)
        k.flush(blk)
        s5.close()

    return nc


def _host_inputs(inputs, core):
    x = np.ascontiguousarray(inputs["L0_attn_x"], dtype=np.float32)
    s, cc = core // 4, core % 4
    xs = np.ascontiguousarray(x[s])
    xown = np.ascontiguousarray(
        xs.reshape(NT, P, D)[cc::4].reshape(OWN * P, D))

    w_dq = np.asarray(inputs["L0_attn_w_dq"], dtype=np.float32)
    w_dkv = np.asarray(inputs["L0_attn_w_dkv"], dtype=np.float32)
    w_uq = np.asarray(inputs["L0_attn_w_uq"], dtype=np.float32)
    wdq_t = np.ascontiguousarray(
        w_dq.reshape(16, P, CQ).transpose(1, 0, 2).reshape(P, 16 * CQ))
    wdkv_t = np.ascontiguousarray(
        w_dkv.reshape(16, P, CKV + KR).transpose(1, 0, 2)
        .reshape(P, 16 * (CKV + KR)))
    perm = np.concatenate(
        [np.concatenate([np.arange(96 * h, 96 * h + 96),
                         1536 + np.arange(32 * h, 32 * h + 32)])
         for h in range(H)])
    wuq_p = np.ascontiguousarray(w_uq[:, perm])

    gq = np.ascontiguousarray(np.broadcast_to(
        np.asarray(inputs["L0_attn_gamma_q"], np.float32)[None, :], (P, CQ)))
    gkv = np.ascontiguousarray(np.broadcast_to(
        np.asarray(inputs["L0_attn_gamma_kv"], np.float32)[None, :],
        (P, CKV)))

    def rope_cs(dd):
        inv = (1.0 / THETA ** (np.arange(0, dd, 2, dtype=np.float32) / dd)
               ).astype(np.float32)
        ang = (np.arange(S, dtype=np.float32)[:, None] * inv[None, :]
               ).astype(np.float32)
        return (np.cos(ang.astype(np.float64)).astype(np.float32),
                np.sin(ang.astype(np.float64)).astype(np.float32))

    def swapform(c, s_):
        n = c.shape[1]
        C2 = np.empty((S, 2 * n), np.float32)
        S2 = np.empty((S, 2 * n), np.float32)
        C2[:, 0::2] = c
        C2[:, 1::2] = c
        S2[:, 0::2] = -s_
        S2[:, 1::2] = s_
        return C2, S2

    ck, sk = rope_cs(KR)
    C2k, S2k = swapform(ck, sk)
    ropeKc = np.ascontiguousarray(
        C2k.reshape(NT, P, KR).transpose(1, 0, 2).reshape(P, NT * KR))
    ropeKs = np.ascontiguousarray(
        S2k.reshape(NT, P, KR).transpose(1, 0, 2).reshape(P, NT * KR))

    cq_, sq_ = rope_cs(QR)
    C2q, S2q = swapform(cq_, sq_)
    own_pos = (np.arange(NT)[cc::4][:, None] * P
               + np.arange(P)[None, :]).reshape(-1)
    C2qo = np.ascontiguousarray(np.tile(C2q[own_pos], (1, H)))
    S2qo = np.ascontiguousarray(np.tile(S2q[own_pos], (1, H)))

    tri = (np.arange(P)[:, None] <= np.arange(P)[None, :]).astype(np.float32)
    m = np.zeros((P, 4 * 512), np.float32)
    for j in range(4):
        pat = (np.ones((P, P), np.float32) if j < cc
               else (tri if j == cc else np.zeros((P, P), np.float32)))
        for hh in range(4):
            m[:, j * 512 + hh * P:j * 512 + (hh + 1) * P] = pat

    return {
        "x_seq": xs,
        "x_own": xown,
        "wdq_t": wdq_t,
        "wdkv_t": wdkv_t,
        "wuq_p": wuq_p,
        "wukv_i": np.ascontiguousarray(
            np.asarray(inputs["L0_attn_w_ukv"], np.float32)),
        "w_o": np.ascontiguousarray(
            np.asarray(inputs["L0_attn_w_o"], np.float32)),
        "w_up": np.ascontiguousarray(
            np.asarray(inputs["L0_ffn_w_up"], np.float32)),
        "w_down": np.ascontiguousarray(
            np.asarray(inputs["L0_ffn_w_down"], np.float32)),
        "gq_bc": gq,
        "gkv_bc": gkv,
        "ropeKc": ropeKc,
        "ropeKs": ropeKs,
        "ropeQc": C2qo,
        "ropeQs": S2qo,
        "mask4": m.astype(BFD),
        "iden_f": np.eye(P).astype(BFD),
        "iden_f32": np.eye(P).astype(np.float32),
        "ones_cb": np.ones((P, 1), BFD),
        "ones_rb": np.ones((1, P), BFD),
        "ones_cf": np.ones((P, 1), np.float32),
        "ones_rf": np.ones((1, P), np.float32),
    }


def kernel(**inputs):
    nc = build_program()
    in_maps = [_host_inputs(inputs, c) for c in range(8)]
    res = run_bass_kernel_spmd(nc, in_maps, core_ids=list(range(8)))
    out = np.empty((B, S, D), np.float32)
    for c in range(8):
        s, cc = c // 4, c % 4
        o = np.asarray(res.results[c]["out_own"]).reshape(OWN, P, D)
        out[s].reshape(NT, P, D)[cc::4] = o
    return out


# version 07
# speedup vs baseline: 1.2894x; 1.0622x over previous
import contextlib
import numpy as np
import ml_dtypes
import concourse.bass as bass
import concourse.mybir as mybir
from concourse.bass_utils import run_bass_kernel_spmd

F32 = mybir.dt.float32
BF16 = mybir.dt.bfloat16
I32 = mybir.dt.int32
AF = mybir.ActivationFunctionType
ALU = mybir.AluOpType
AX = mybir.AxisListType
BFD = ml_dtypes.bfloat16

B, S, D = 2, 4096, 2048
H, NKV = 16, 4
CQ, QN, QR = 128, 96, 32
CKV, KR = 128, 64
KN, VD = 64, 256
THETA = 1e4
EPS = 1e-5
CSC = float(1.0 / np.sqrt(128.0))
NEGB_MUL = float(-np.sqrt(128.0))
P = 128
NT = S // P
OWN = 8
IFF = 2 * D

ENGINES = ("sync", "tensor", "vector", "scalar", "gpsimd")


class Kern:
    ENGSEM = {"vector": "s_dve", "scalar": "s_act", "gpsimd": "s_pool"}

    def __init__(self, nc):
        self.nc = nc
        self.plan = {e: [] for e in ENGINES}
        self.hwm = {e: {} for e in ENGINES}
        self.cnt = {}
        self.sems = {}
        self.tok = {}
        self.prev = {}

    def add_sem(self, ctx, name):
        self.sems[name] = ctx.enter_context(self.nc.semaphore(name))
        self.cnt[name] = 0

    def emit(self, eng, fn, waits=(), inc=None, inc_by=1, chain=True):
        flat = []
        for w in waits:
            if w is None:
                continue
            if isinstance(w, list):
                flat.extend(x for x in w if x is not None)
            else:
                flat.append(w)
        if chain and eng in self.ENGSEM and inc_by == 1:
            es = self.ENGSEM[eng]
            if self.prev.get(eng) is not None:
                flat.append(self.prev[eng])
            if inc is None:
                inc = es
        t = None
        if inc is not None:
            self.cnt[inc] += inc_by
            t = (inc, self.cnt[inc])
        if chain and eng in self.ENGSEM and inc_by == 1:
            self.prev[eng] = t
        self.plan[eng].append((tuple(flat), fn, inc, inc_by, t))
        return t

    def flush(self, block):
        for eng in ENGINES:
            ops = self.plan[eng]
            if not ops:
                continue
            hwm = self.hwm[eng]
            sems = self.sems

            def body(e, ops=ops, hwm=hwm, sems=sems):
                for waits, fn, inc, inc_by, t in ops:
                    for (sn, val) in waits:
                        if hwm.get(sn, 0) < val:
                            e.wait_ge(sems[sn], val)
                            hwm[sn] = val
                    last = fn(e)
                    if inc is not None:
                        assert last is not None
                        last.then_inc(sems[inc], inc_by)

            getattr(block, eng)(body)
        self.plan = {e: [] for e in ENGINES}

    def dma(self, q, dst, src, waits=(), sem=None):
        sem = sem or {"sync": "s_ld", "scalar": "s_l2", "gpsimd": "s_st"}[q]
        prior = (sem, self.cnt[sem])
        return self.emit(q, lambda e: e.dma_start(out=dst, in_=src),
                         tuple(waits) + (prior,), inc=sem, inc_by=16)

    def mm(self, ps, lhsT, rhs, start, stop, waits=(), inc=None):
        return self.emit(
            "tensor",
            lambda e: e.matmul(ps, lhsT, rhs, start=start, stop=stop),
            waits, inc=inc)

    def tr(self, ps, src, iden, waits=(), inc=None):
        return self.emit("tensor", lambda e: e.transpose(ps, src, iden),
                         waits, inc=inc)

    def cp(self, eng, dst, src, waits=(), inc=None):
        return self.emit(eng, lambda e: e.tensor_copy(dst, src), waits,
                         inc=inc)

    def act(self, dst, src, func, waits=(), inc=None, scale=None, bias=None,
            accum_out=None):
        kw = {}
        if scale is not None:
            kw["scale"] = scale
        if bias is not None:
            kw["bias"] = bias
        if accum_out is not None:
            kw["accum_out"] = accum_out
        return self.emit("scalar",
                         lambda e: e.activation(dst, src, func, **kw),
                         waits, inc=inc)

    def ts(self, eng, dst, src, s1, s2, op0, op1, waits=(), inc=None):
        if op1 is None:
            return self.emit(
                eng,
                lambda e: e.tensor_scalar(dst, src, s1, None, op0=op0),
                waits, inc=inc)
        return self.emit(
            eng,
            lambda e: e.tensor_scalar(dst, src, s1, s2, op0=op0, op1=op1),
            waits, inc=inc)

    def tt(self, eng, dst, a, b, op, waits=(), inc=None):
        return self.emit(eng, lambda e: e.tensor_tensor(dst, a, b, op=op),
                         waits, inc=inc)

    def stt(self, eng, dst, a, sc, b, op0, op1, waits=(), inc=None):
        return self.emit(
            eng,
            lambda e: e.scalar_tensor_tensor(dst, a, sc, b, op0=op0, op1=op1),
            waits, inc=inc)

    def barrier(self):
        toks = [(sn, self.cnt[sn]) for sn in self.cnt]
        for eng in ENGINES:
            self.emit(eng, lambda e: None, waits=tuple(toks), chain=False)

    def recip(self, dst, src, waits=(), inc=None):
        return self.emit("vector", lambda e: e.reciprocal(dst, src),
                         waits, inc=inc)

    def red(self, dst, src, op, absval=False, waits=(), inc=None):
        if absval:
            return self.emit(
                "vector",
                lambda e: e.tensor_reduce(dst, src, axis=AX.X, op=op,
                                          apply_absolute_value=True),
                waits, inc=inc)
        return self.emit(
            "vector",
            lambda e: e.tensor_reduce(dst, src, axis=AX.X, op=op),
            waits, inc=inc)


def ap3(tensor, off, dims):
    return bass.AP(tensor, off, dims)


def build_program(stop=99, n1=NT, parts=65535, lnlvl=9, dbg=False):
    nc = bass.Bass("TRN2", target_bir_lowering=False,
                   detect_race_conditions=False)

    def din(name, shape, dtype=F32):
        return nc.dram_tensor(name, list(shape), dtype,
                              kind="ExternalInput").ap()

    x_seq = din("x_seq", (S, D))
    x_own = din("x_own", (OWN * P, D))
    wdq_t = din("wdq_t", (P, 16 * CQ))
    wdkv_t = din("wdkv_t", (P, 16 * (CKV + KR)))
    wuq_p = din("wuq_p", (CQ, 2048))
    wukv_i = din("wukv_i", (CKV, 1280))
    w_o = din("w_o", (H * VD, D))
    w_up = din("w_up", (IFF, D))
    w_down = din("w_down", (D, IFF))
    gq_bc = din("gq_bc", (P, CQ))
    gkv_bc = din("gkv_bc", (P, CKV))
    ropeKc = din("ropeKc", (P, NT * KR))
    ropeKs = din("ropeKs", (P, NT * KR))
    ropeQc = din("ropeQc", (OWN * P, H * QR))
    ropeQs = din("ropeQs", (OWN * P, H * QR))
    mask4 = din("mask4", (P, 4 * 512), BF16)
    iden_f = din("iden_f", (P, P), BF16)
    iden_f32 = din("iden_f32", (P, P))
    ones_cb = din("ones_cb", (P, 1), BF16)
    ones_rb = din("ones_rb", (1, P), BF16)
    ones_cf = din("ones_cf", (P, 1))
    ones_rf = din("ones_rf", (1, P))

    out_own = nc.dram_tensor("out_own", [OWN * P, D], F32,
                             kind="ExternalOutput").ap()
    dbg_x1 = dbg_of = dbg_wu = dbg_sc = None
    dbg_kv = None
    if dbg:
        dbg_sc = nc.dram_tensor("dbg_sc", [P, 66], F32,
                                kind="ExternalOutput").ap()
        dbg_kv = nc.dram_tensor("dbg_kv", [P, 3 * 8192], BF16,
                                kind="ExternalOutput").ap()
        dbg_x1 = nc.dram_tensor("dbg_x1", [OWN * P, D], F32,
                                kind="ExternalOutput").ap()
        dbg_of = nc.dram_tensor("dbg_of", [OWN * P, H * VD], BF16,
                                kind="ExternalOutput").ap()
        dbg_wu = nc.dram_tensor("dbg_wu", [D, IFF], BF16,
                                kind="ExternalOutput").ap()
    outf_dram = nc.dram_tensor("outf_dram", [OWN * P, H * VD], BF16,
                               kind="Internal").ap()
    x1_dram = nc.dram_tensor("x1_dram", [OWN * P, D], F32,
                             kind="Internal").ap()
    wo_bf_dram = nc.dram_tensor("wo_bf_dram", [H * VD, D], BF16,
                                kind="Internal").ap()
    wupT_d = nc.dram_tensor("wupT_d", [D, IFF], BF16, kind="Internal")
    wdnT_d = nc.dram_tensor("wdnT_d", [IFF, D], BF16, kind="Internal")

    with contextlib.ExitStack() as top:
        k = Kern(nc)
        for sname in ("s_ld", "s_l2", "s_st", "s_pe", "s_dve", "s_act",
                      "s_pool", "s_x0", "s_x1", "s_stg", "s_wq", "s_wo0",
                      "s_wo1", "s_of"):
            k.add_sem(top, sname)

        def mk(ctx, name, shape, dtype):
            return ctx.enter_context(nc.sbuf_tensor(name, list(shape), dtype))

        iden = mk(top, "iden", (P, P), BF16)
        idenf = mk(top, "idenf", (P, P), F32)
        onescb = mk(top, "onescb", (P, 1), BF16)
        onesrb = mk(top, "onesrb", (1, P), BF16)
        onescf = mk(top, "onescf", (P, 1), F32)
        onesrf = mk(top, "onesrf", (1, P), F32)
        vec = mk(top, "vec", (P, 32), F32)
        sca = mk(top, "sca", (1, 32), F32)
        wsum = mk(top, "wsum", (P, 64), F32)
        negB = mk(top, "negB", (P, 1), F32)
        supv = mk(top, "supv", (P, 1), F32)
        rsupv = mk(top, "rsupv", (P, 1), F32)
        sdnv = mk(top, "sdnv", (P, 1), F32)
        rsdnv = mk(top, "rsdnv", (P, 1), F32)
        srow1 = mk(top, "srow1", (P, OWN), F32)
        srow2 = mk(top, "srow2", (P, OWN), F32)
        dumpb = mk(top, "dumpb", (P, D), BF16)

        blk = top.enter_context(nc.Block())

        k.emit("gpsimd", lambda e: e.memset(vec[:], 0.0), inc="s_pool")
        t_vz = ("s_pool", k.cnt["s_pool"])
        k.emit("gpsimd", lambda e: e.memset(sca[:], 0.0), inc="s_pool")
        t_sz = ("s_pool", k.cnt["s_pool"])

        vecfree = [t_vz]

        def ln_chain(src_ps_or_sb, n, gamma_bc, out_dst, mmt, extra=(),
                     app_extra=()):
            if lnlvl == 16:
                tsq = k.act(dumpb[:, 0:min(n, D)], src_ps_or_sb, AF.Square,
                            waits=(mmt,), inc="s_act")
                vecfree[0] = tsq
                return tsq, tsq, tsq
            tred = k.red(vec[:, 0:1], src_ps_or_sb, ALU.add,
                         waits=(mmt, vecfree[0]) + tuple(extra),
                         inc="s_dve")
            if lnlvl < 1:
                vecfree[0] = tred
                return tred, tred, tred
            if lnlvl == 11:
                tsq = k.act(dumpb[:, 0:min(n, D)], src_ps_or_sb, AF.Square,
                            waits=(mmt, tred), inc="s_act")
                vecfree[0] = tsq
                return tred, tsq, tred
            if lnlvl == 12:
                tsq = k.act(dumpb[:, 0:min(n, D)], src_ps_or_sb, AF.Copy,
                            accum_out=vec[:, 1:2], waits=(mmt, tred),
                            inc="s_act")
                vecfree[0] = tsq
                return tred, tsq, tred
            tsq = k.act(dumpb[:, 0:min(n, D)], src_ps_or_sb, AF.Square,
                        accum_out=vec[:, 1:2], waits=(mmt, tred),
                        inc="s_act")
            if lnlvl < 2:
                vecfree[0] = tsq
                return tred, tsq, tred
            k.ts("vector", vec[:, 2:3], vec[:, 0:1], 1.0 / n, None,
                 ALU.mult, None)
            k.ts("vector", vec[:, 3:4], vec[:, 1:2], 1.0 / n, None,
                 ALU.mult, None, waits=(tsq,))
            k.tt("vector", vec[:, 4:5], vec[:, 2:3], vec[:, 2:3], ALU.mult)
            tvar = k.stt("vector", vec[:, 4:5], vec[:, 3:4], float(EPS),
                         vec[:, 4:5], ALU.add, ALU.subtract, inc="s_dve")
            if lnlvl < 3:
                vecfree[0] = tsq
                return tvar, tsq, tvar
            k.act(vec[:, 5:6], vec[:, 4:5], AF.Ln, waits=(tvar,))
            trst = k.act(vec[:, 5:6], vec[:, 5:6], AF.Exp, scale=-0.5,
                         inc="s_act")
            if lnlvl < 4:
                vecfree[0] = trst
                return tvar, trst, tvar
            tmr = k.tt("vector", vec[:, 6:7], vec[:, 2:3], vec[:, 5:6],
                       ALU.mult, waits=(trst,), inc="s_dve")
            if lnlvl < 5:
                vecfree[0] = trst
                return tvar, trst, tmr
            tapp = k.act(out_dst, src_ps_or_sb, AF.Copy, scale=vec[:, 5:6],
                         waits=(trst,) + tuple(app_extra), inc="s_act")
            tsub = k.ts("vector", out_dst, out_dst, vec[:, 6:7], None,
                        ALU.subtract, None, waits=(tapp, tmr), inc="s_dve")
            vecfree[0] = tapp
            return tvar, tapp, tsub

        s13 = contextlib.ExitStack()
        Kf = [mk(s13, f"kf{g}", (P, S), BF16) for g in range(NKV)]
        Vt = [mk(s13, f"vt{g}", (P, NT * VD), BF16) for g in range(NKV)]

        s12 = contextlib.ExitStack()
        wdqb = mk(s12, "wdqb", (P, 16 * CQ), BF16)
        wuqb = mk(s12, "wuqb", (CQ, 2048), BF16)
        xb = [mk(s12, f"xb{i}", (P, D), BF16) for i in range(2)]
        xfb = [mk(s12, f"xfb{i}", (P, D), BF16) for i in range(2)]
        cvn = mk(s12, "cvn", (P, P), BF16)

        tC = {}
        for nm, dst, src in [("iden", iden[:], iden_f),
                             ("idenf", idenf[:], iden_f32),
                             ("onescb", onescb[:], ones_cb),
                             ("onesrb", onesrb[:], ones_rb),
                             ("onescf", onescf[:], ones_cf),
                             ("onesrf", onesrf[:], ones_rf)]:
            tC[nm] = k.dma("gpsimd", dst, src)
        t_wdqb = k.dma("gpsimd", wdqb[:], wdq_t)
        t_wuqb = k.dma("gpsimd", wuqb[:], wuq_p)
        t_wocast = None
        if parts & 128:
            t_wocast = k.dma("gpsimd", wo_bf_dram, w_o)

        s1 = contextlib.ExitStack()
        wdkvb = mk(s1, "wdkvb", (P, 16 * (CKV + KR)), BF16)
        wukvb = mk(s1, "wukvb", (CKV, 1280), BF16)
        rkc = mk(s1, "rkc", (P, NT * KR), F32)
        rks = mk(s1, "rks", (P, NT * KR), F32)
        knsc = mk(s1, "knsc", (P, 320), BF16)
        scrA = mk(s1, "scrA1", (P, D), F32)
        scrB = mk(s1, "scrB1", (P, D), F32)
        gkvbc = mk(s1, "gkvbc", (P, CKV), F32)
        ckvf = mk(s1, "ckvf", (P, P), BF16)
        wlf = [mk(s1, f"wlf{i}", (P, D), F32) for i in range(2)]
        pdkv = [s1.enter_context(nc.psum_tensor(f"pdkv{i}", [P, 192], F32))
                for i in range(2)]
        pkv = [s1.enter_context(nc.psum_tensor("pkv0", [P, 512], F32)),
               s1.enter_context(nc.psum_tensor("pkv1", [P, 512], F32)),
               s1.enter_context(nc.psum_tensor("pkv2", [P, 256], F32))]
        ptr = [s1.enter_context(nc.psum_tensor(f"ptr{i}", [P, 1024], BF16))
               for i in range(2)]
        psc = s1.enter_context(nc.psum_tensor("psc", [P, 4], F32))

        if stop <= 0:
            k.flush(blk)
            s1.close(); s12.close(); s13.close()
            return nc
        t_wdkvb = k.dma("gpsimd", wdkvb[:], wdkv_t)
        t_wukvb = k.dma("gpsimd", wukvb[:], wukv_i)
        t_rkc = k.dma("gpsimd", rkc[:], ropeKc)
        t_rks = k.dma("gpsimd", rks[:], ropeKs)
        t_gkv = k.dma("gpsimd", gkvbc[:], gkv_bc)

        if not (parts & 1):
            k.emit("gpsimd", lambda e: e.memset(xfb[0][:], 0.01),
                   inc="s_pool")
            k.emit("gpsimd", lambda e: e.memset(xfb[1][:], 0.01),
                   inc="s_pool")
        t_xfmem = ("s_pool", k.cnt["s_pool"])
        tok_x_free = [None, None]
        tok_xfb_free = [None, None]
        tok_tt1 = None
        tok_tt4 = None
        tok_tgam_prev = None
        tok_dkv_free = [[], []]
        tok_kv_free = []
        tok_ptr_free = [None, None]
        tok_wl_free = [None, None]
        tok_kdone = None

        def p1_front(t):
            par = t % 2
            if not (parts & 1):
                txf = t_xfmem
                if not (parts & 2):
                    return None
                mmt = None
                for dc in range(16):
                    mmt = k.mm(pdkv[par][:],
                               xfb[par][:, dc * 128:(dc + 1) * 128],
                               wdkvb[:, dc * 192:(dc + 1) * 192],
                               start=(dc == 0), stop=(dc == 15),
                               waits=(txf, t_wdkvb, tok_dkv_free[par]),
                               inc="s_pe" if dc == 15 else None)
                tok_xfb_free[par] = mmt
            else:
                tx = k.dma("gpsimd", xb[par][:], x_seq[t * P:(t + 1) * P, :],
                           waits=(tok_x_free[par],), sem=f"s_x{par}")
                for half in range(2):
                    bank = ptr[1]
                    trt = None
                    for bnum in range(8):
                        dc = half * 8 + bnum
                        trt = k.tr(bank[:, bnum * 128:(bnum + 1) * 128],
                                   xb[par][:, dc * 128:(dc + 1) * 128],
                                   iden[:],
                                   waits=(tx, tC["iden"], tok_ptr_free[1]),
                                   inc="s_pe" if bnum == 7 else None)
                    tok_ptr_free[1] = k.cp(
                        "vector",
                        xfb[par][:, half * 1024:(half + 1) * 1024],
                        bank[:], waits=(trt, tok_xfb_free[par]),
                        inc="s_dve")
                tok_x_free[par] = ("s_pe", k.cnt["s_pe"])
                txf = tok_ptr_free[1]

                if not (parts & 2):
                    return None
                mmt = None
                for dc in range(16):
                    mmt = k.mm(pdkv[par][:],
                               xfb[par][:, dc * 128:(dc + 1) * 128],
                               wdkvb[:, dc * 192:(dc + 1) * 192],
                               start=(dc == 0), stop=(dc == 15),
                               waits=(txf, t_wdkvb, tok_dkv_free[par]),
                               inc="s_pe" if dc == 15 else None)
                tok_xfb_free[par] = mmt
            return mmt if (parts & 2) else None

        def p1_back(t, mmt):
            nonlocal tok_kv_free, tok_kdone, tok_tt1, tok_tt4
            nonlocal tok_tgam_prev
            par = t % 2
            if not (parts & 4) or mmt is None:
                return
            ps = pdkv[par]
            tkr0 = k.cp("vector", scrA[:, 256:256 + KR],
                        ps[:, CKV:CKV + KR], waits=(mmt,))
            tvar, tapp, tsub = ln_chain(ps[:, 0:CKV], CKV, None,
                                        scrA[:, 0:CKV], mmt,
                                        app_extra=(tok_tgam_prev,))
            if lnlvl >= 11:
                tok_dkv_free[par] = [tapp, tvar]
                return
            tgam = k.tt("vector", cvn[:], scrA[:, 0:CKV], gkvbc[:],
                        ALU.mult, waits=(t_gkv, tsub), inc="s_dve")

            if not (parts & 32):
                tok_dkv_free[par] = [tapp]
                return
            k.cp("vector", ap3(scrA, 320, [[D, P], [2, KR // 2]]),
                 ap3(scrA, 257, [[D, P], [2, KR // 2]]))
            tsw = k.cp("vector", ap3(scrA, 321, [[D, P], [2, KR // 2]]),
                       ap3(scrA, 256, [[D, P], [2, KR // 2]]), inc="s_dve")
            k.tt("vector", scrB[:, 0:KR], scrA[:, 256:256 + KR],
                 rkc[:, t * KR:(t + 1) * KR], ALU.mult,
                 waits=(t_rkc, t_rks))
            k.tt("vector", scrB[:, 64:128], scrA[:, 320:320 + KR],
                 rks[:, t * KR:(t + 1) * KR], ALU.mult, waits=(tsw,))
            trr = k.tt("vector", knsc[:, 256:320], scrB[:, 0:64],
                       scrB[:, 64:128], ALU.add, waits=(tok_tt4,),
                       inc="s_dve")
            tok_dkv_free[par] = [trr, tapp]

            if not (parts & 8):
                return
            tt1 = k.tr(ptr[0][:, 0:128], cvn[:], iden[:],
                       waits=(tgam, tok_ptr_free[0]), inc="s_pe")
            tok_tt1 = tt1
            tckvf = k.cp("vector", ckvf[:], ptr[0][:, 0:128], waits=(tt1,),
                         inc="s_dve")

            k.mm(pkv[0][:], ckvf[:], wukvb[:, 0:512], True, True,
                 waits=[tckvf, t_wukvb] + tok_kv_free)
            k.mm(pkv[1][:], ckvf[:], wukvb[:, 512:1024], True, True)
            tkv = k.mm(pkv[2][:], ckvf[:], wukvb[:, 1024:1280], True, True,
                       inc="s_pe")
            k.cp("vector", Vt[0][:, t * VD:(t + 1) * VD], pkv[0][:, 256:512],
                 waits=(tkv,))
            k.cp("vector", Vt[1][:, t * VD:(t + 1) * VD], pkv[1][:, 0:256])
            k.cp("vector", Vt[2][:, t * VD:(t + 1) * VD], pkv[1][:, 256:512])
            k.cp("vector", Vt[3][:, t * VD:(t + 1) * VD], pkv[2][:])
            tkn = k.cp("vector", knsc[:, 0:256], pkv[0][:, 0:256],
                       inc="s_dve")
            tok_kv_free = [tkn]
            k.tr(ptr[0][:, 128:256], knsc[:, 0:128], iden[:],
                 waits=(tkn, tckvf), inc="s_pe")
            k.tr(ptr[0][:, 256:384], knsc[:, 128:256], iden[:], inc="s_pe")
            tt4 = k.tr(ptr[0][0:64, 384:512], knsc[:, 256:320], iden[:],
                       waits=(trr,), inc="s_pe")
            tok_tt4 = tt4
            k.cp("vector", Kf[0][0:64, t * P:(t + 1) * P],
                 ptr[0][0:64, 128:256], waits=(tt4,))
            k.cp("vector", Kf[1][0:64, t * P:(t + 1) * P],
                 ptr[0][64:128, 128:256])
            k.cp("vector", Kf[2][0:64, t * P:(t + 1) * P],
                 ptr[0][0:64, 256:384])
            k.cp("vector", Kf[3][0:64, t * P:(t + 1) * P],
                 ptr[0][64:128, 256:384])
            for g in range(3):
                k.cp("vector", Kf[g][64:128, t * P:(t + 1) * P],
                     ptr[0][0:64, 384:512])
            tkall = k.cp("vector", Kf[3][64:128, t * P:(t + 1) * P],
                         ptr[0][0:64, 384:512], inc="s_dve")
            tok_ptr_free[0] = tkall
            tok_kdone = tkall

            if not (parts & 16):
                return
            wpar = t % 2
            tu = k.dma("scalar", wlf[wpar][:], w_up[t * P:(t + 1) * P, :],
                       waits=(tok_wl_free[wpar],))
            tau = k.act(dumpb[:], wlf[wpar][:], AF.Abs,
                        accum_out=wsum[:, t:t + 1], waits=(tu,), inc="s_act")
            rr, cc_ = t % 16, t // 16
            td = k.dma("scalar", wlf[wpar][:],
                       w_down[rr * P:(rr + 1) * P, cc_ * D:(cc_ + 1) * D],
                       waits=(tau,))
            tad = k.act(dumpb[:], wlf[wpar][:], AF.Abs,
                        accum_out=wsum[:, 32 + t:33 + t], waits=(td,),
                        inc="s_act")
            tok_wl_free[wpar] = tad

        if not (parts & 64):
            k.barrier()
            k.flush(blk)
            s1.close()
            if stop <= 1:
                s12.close(); s13.close()
                return nc

        mm_prev = p1_front(0) if n1 > 0 else None
        for t in range(n1):
            mm_next = p1_front(t + 1) if t + 1 < n1 else None
            p1_back(t, mm_prev)
            mm_prev = mm_next
        tsu = k.red(vec[:, 8:9], wsum[:, 0:32], ALU.add,
                    waits=(("s_act", k.cnt["s_act"]), vecfree[0]),
                    inc="s_dve")
        tsd = k.red(vec[:, 10:11], wsum[:, 32:64], ALU.add, inc="s_dve")
        m1 = k.mm(psc[0:1, 0:2], onescf[:], vec[:, 8:10], True, True,
                  waits=(tsu, tC["onescf"]), inc="s_pe")
        tc1 = k.ts("vector", sca[0:1, 0:1], psc[0:1, 0:1],
                   1.0 / float(IFF * D), float(EPS), ALU.mult, ALU.max,
                   waits=(m1, t_sz), inc="s_dve")
        m2 = k.mm(psc[0:1, 2:4], onescf[:], vec[:, 10:12], True, True,
                  waits=(tsd, tc1), inc="s_pe")
        tc2 = k.ts("vector", sca[0:1, 4:5], psc[0:1, 2:3],
                   1.0 / float(IFF * D), float(EPS), ALU.mult, ALU.max,
                   waits=(m2,), inc="s_dve")
        k.recip(sca[0:1, 2:3], sca[0:1, 0:1], waits=(tc1,))
        tr2 = k.recip(sca[0:1, 6:7], sca[0:1, 4:5], waits=(tc2,),
                      inc="s_dve")
        t_scal = None
        for i, (dstv, col) in enumerate(((supv, 0), (rsupv, 2), (sdnv, 4),
                                         (rsdnv, 6))):
            mb = k.mm(psc[:, 0:2], onesrf[:], sca[0:1, col:col + 2],
                      True, True, waits=(tr2, tC["onesrf"], tc2, t_scal),
                      inc="s_pe")
            t_scal = k.cp("vector", dstv[:], psc[:, 0:1], waits=(mb,),
                          inc="s_dve")

        tprev = vecfree[0]
        first = True
        for g in range(NKV):
            for hf in range(2):
                tsqk = k.act(dumpb[:], Kf[g][:, hf * D:(hf + 1) * D],
                             AF.Square, waits=(tok_kdone, tprev),
                             inc="s_act")
                for ch in range(4):
                    mm_ = k.mm(pkv[1][0:1, 0:512], onescb[:],
                               dumpb[:, ch * 512:(ch + 1) * 512],
                               True, True,
                               waits=[tsqk, tC["onescb"], tok_kv_free,
                                      tprev], inc="s_pe")
                    tred_ = k.red(sca[0:1, 11:12], pkv[1][0:1, 0:512],
                                  ALU.max, waits=(mm_,), inc="s_dve")
                    tok_kv_free = [tred_]
                    if first:
                        tprev = k.cp("vector", sca[0:1, 8:9],
                                     sca[0:1, 11:12], inc="s_dve")
                        first = False
                    else:
                        tprev = k.tt("vector", sca[0:1, 8:9],
                                     sca[0:1, 8:9], sca[0:1, 11:12],
                                     ALU.max, inc="s_dve")
        tmk2 = tprev
        k.barrier()
        k.flush(blk)
        if dbg and stop <= 1:
            k.dma("gpsimd", dbg_kv[:, 0:4096], Kf[0][:],
                  waits=[("s_pe", k.cnt["s_pe"]),
                         ("s_dve", k.cnt["s_dve"]),
                         ("s_act", k.cnt["s_act"]),
                         ("s_pool", k.cnt["s_pool"])])
            k.dma("gpsimd", dbg_kv[:, 4096:12288], Vt[0][:])
            k.dma("gpsimd", dbg_kv[:, 12288:12608], knsc[:])
            k.dma("gpsimd", dbg_kv[:, 12608:12736], cvn[:])
            k.dma("gpsimd", dbg_kv[:, 12736:14784], xfb[1][:])
            k.dma("gpsimd", dbg_kv[:, 14784:14912], ckvf[:])
            tpcp = k.cp("vector", scrB[:, 0:192], pdkv[n1 % 2 ^ 1][:],
                        inc="s_dve")
            tvcp = k.cp("vector", scrB[:, 192:224], vec[:], waits=(tpcp,),
                        inc="s_dve")
            k.dma("gpsimd", dbg_kv[:, 14912:15360],
                  scrB.bitcast(BF16)[:, 0:448], waits=(tvcp,))
            k.emit("gpsimd", lambda e: None,
                   waits=(("s_st", k.cnt["s_st"]),), chain=False)
            k.flush(blk)
        s1.close()
        if stop <= 1:
            s12.close(); s13.close()
            return nc

        s23 = contextlib.ExitStack()
        Q4 = [mk(s23, f"q4_{g}", (P, 4 * 512), BF16) for g in range(NKV)]
        E4 = [mk(s23, f"e4_{i}", (P, 512), BF16) for i in range(2)]
        outfb = [mk(s23, "outfb0", (P, H * VD), BF16)]
        masks = mk(s23, "masks", (P, 4 * 512), BF16)
        l32 = mk(s23, "l32", (1, 512), F32)
        reclb = mk(s23, "reclb", (1, 512), BF16)
        bcb = mk(s23, "bcb", (P, 512), BF16)
        scrA2 = mk(s23, "scrA2", (P, D), F32)
        scrB2 = mk(s23, "scrB2", (P, D), F32)
        rqc = mk(s23, "rqc", (P, 512), F32)
        rqs = mk(s23, "rqs", (P, 512), F32)
        gqbc = mk(s23, "gqbc", (P, CQ), F32)
        cqf = mk(s23, "cqf", (P, P), BF16)
        wqf = [mk(s23, "wqf0", (P, D), F32)]
        scrI = wqf[0].bitcast(I32)
        wqb = mk(s23, "wqb", (P, D), BF16)
        wqstage = mk(s23, "wqstage", (P, 16 * 256), BF16)

        t_masks = k.dma("gpsimd", masks[:], mask4)
        t_gq = k.dma("gpsimd", gqbc[:], gq_bc)

        pso = [s23.enter_context(nc.psum_tensor(f"pso{i}", [P, 512], F32))
               for i in range(4)]
        plq = s23.enter_context(nc.psum_tensor("plq", [1, 512], F32))
        pbc = s23.enter_context(nc.psum_tensor("pbc", [P, 512], F32))
        ptrw = s23.enter_context(nc.psum_tensor("ptrw", [P, 1024], BF16))

        bankfree = {i: [] for i in range(4)}
        tok_plq_free = []
        tok_pbc_free = []
        tok_ptrw_free = []
        tok_q4_free = [[] for _ in range(NKV)]
        tok_E_free = [None, None]
        tok_outf_free = [None, None]
        tok_wq_ld_free = [None, None]
        tok_stage_free = []
        tok_scrP_free = None
        tok_scrI_free = None
        tok_wqb_free = None
        tok_negb_rd = None
        tok_xq_free = [None, None]
        tok_xfq_free = [None, None]
        tok_cvn_free = None
        tok_cqf_free = None
        tok_rq_free = None

        nquant = [0]

        def quant_slot():
            for _ in range(2):
                j = nquant[0]
                if j >= 64:
                    return
                nquant[0] += 1
                nonlocal_tok = None
                up = j < 32
                jj = j if up else j - 32
                wpar = 0
                if up:
                    src = w_up[jj * P:(jj + 1) * P, :]
                    rsv = rsupv
                    it4 = jj % 2
                    grp = jj // 2
                else:
                    rr, cc_ = jj % 16, jj // 16
                    src = w_down[rr * P:(rr + 1) * P, cc_ * D:(cc_ + 1) * D]
                    rsv = rsdnv
                    it4 = rr % 2
                    grp = rr // 2
                tl = k.dma("scalar", wqf[wpar][:], src,
                           waits=(tok_wq_ld_free[wpar], t_scal),
                           sem="s_wq")
                ti_ = k.ts("vector", scrI[:], wqf[wpar][:], rsv[:], None,
                           ALU.mult, None, waits=(tl,), inc="s_dve")
                tc = k.ts("gpsimd", wqb[:], scrI[:], 1.0, -1.0, ALU.min,
                          ALU.max, waits=(ti_, tok_wqb_free), inc="s_pool")
                tok_wq_ld_free[wpar] = tc
                tok_scrI_free2 = tc
                tok_scrP_free2 = None
                trd = None
                for rnd in range(2):
                    trt = None
                    for bnum in range(8):
                        dc = rnd * 8 + bnum
                        trt = k.tr(ptrw[:, bnum * 128:(bnum + 1) * 128],
                                   wqb[:, dc * 128:(dc + 1) * 128], iden[:],
                                   waits=[tc] + tok_ptrw_free,
                                   inc="s_pe" if bnum == 7 else None)
                    dst = ap3(wqstage, it4 * 128 + rnd * 8 * 256,
                              [[16 * 256, P], [256, 8], [1, 128]])
                    trd = k.cp("vector", dst, ptrw[:],
                               waits=[trt] + tok_stage_free, inc="s_dve")
                    tok_ptrw_free.clear()
                    tok_ptrw_free.append(trd)
                tok_wqb_free2 = ("s_pe", k.cnt["s_pe"])
                nonlocal_tok = (tok_scrP_free2, tok_scrI_free2,
                                tok_wqb_free2)
                _set_quant_toks(nonlocal_tok)
                if it4 == 1:
                    base = grp * 256
                    if up:
                        dsl = ap3(wupT_d, base,
                                  [[IFF, P], [P * IFF, 16], [1, 256]])
                    else:
                        dsl = ap3(wdnT_d, cc_ * D * D + base,
                                  [[D, P], [P * D, 16], [1, 256]])
                    tout = k.dma("gpsimd", dsl, wqstage[:], waits=(trd,),
                                 sem="s_stg")
                    tok_stage_free.clear()
                    tok_stage_free.append(tout)

        def _set_quant_toks(toks):
            nonlocal tok_scrP_free, tok_scrI_free, tok_wqb_free
            tok_scrP_free, tok_scrI_free, tok_wqb_free = toks

        for quad in range(2):
            t_q4_last = None
            for kin in range(4):
                kq = quad * 4 + kin
                par = kin % 2
                tx = k.dma("gpsimd", xb[par][:],
                           x_own[kq * P:(kq + 1) * P, :],
                           waits=(tok_xq_free[par],), sem=f"s_x{par}")
                trc = k.dma("sync", rqc[:], ropeQc[kq * P:(kq + 1) * P, :],
                            waits=(tok_rq_free,))
                trs = k.dma("sync", rqs[:], ropeQs[kq * P:(kq + 1) * P, :])
                for half in range(2):
                    trt = None
                    for bnum in range(8):
                        dc = half * 8 + bnum
                        trt = k.tr(ptrw[:, bnum * 128:(bnum + 1) * 128],
                                   xb[par][:, dc * 128:(dc + 1) * 128],
                                   iden[:],
                                   waits=[tx] + tok_ptrw_free,
                                   inc="s_pe" if bnum == 7 else None)
                    tev = k.cp(
                        "vector", xfb[par][:, half * 1024:(half + 1) * 1024],
                        ptrw[:], waits=(trt, tok_xfq_free[par]),
                        inc="s_dve")
                    tok_ptrw_free.clear()
                    tok_ptrw_free.append(tev)
                tok_xq_free[par] = ("s_pe", k.cnt["s_pe"])
                txf = tok_ptrw_free[0]
                mmt = None
                for dc in range(16):
                    mmt = k.mm(pso[0][:, 0:128],
                               xfb[par][:, dc * 128:(dc + 1) * 128],
                               wdqb[:, dc * 128:(dc + 1) * 128],
                               start=(dc == 0), stop=(dc == 15),
                               waits=[txf, t_wdqb] + bankfree[0],
                               inc="s_pe" if dc == 15 else None)
                tok_xfq_free[par] = mmt
                tvar, tapp, tsub = ln_chain(pso[0][:, 0:128], CQ, None,
                                            scrA2[:, 0:CQ], mmt,
                                            extra=(tok_cvn_free,),
                                            app_extra=(k.tok.get("p2tgam"),))
                bankfree[0] = [tvar, tapp]
                tgam = k.tt("vector", cvn[:], scrA2[:, 0:CQ], gqbc[:],
                            ALU.mult, waits=(t_gq, tsub, tok_cvn_free),
                            inc="s_dve")
                k.tok["p2tgam"] = tgam
                ttq = k.tr(ptrw[:, 0:128], cvn[:], iden[:],
                           waits=[tgam] + tok_ptrw_free, inc="s_pe")
                tok_cvn_free = ttq
                tcqf = k.cp("vector", cqf[:], ptrw[:, 0:128],
                            waits=(ttq, tok_cqf_free), inc="s_dve")
                tok_ptrw_free.clear()
                tok_ptrw_free.append(tcqf)
                for c4 in range(4):
                    bk = 1 + (c4 % 2)
                    mmq = k.mm(pso[bk][:], cqf[:],
                               wuqb[:, c4 * 512:(c4 + 1) * 512], True, True,
                               waits=[tcqf, t_wuqb] + bankfree[bk],
                               inc="s_pe")
                    bankfree[bk] = [k.cp(
                        "vector", scrA2[:, c4 * 512:(c4 + 1) * 512],
                        pso[bk][:], waits=(mmq,), inc="s_dve")]
                tok_cqf_free = ("s_pe", k.cnt["s_pe"])
                tq32 = bankfree[2][0]
                xv = ap3(scrA2, 96, [[D, P], [128, 16], [1, 32]])
                xe = ap3(scrA2, 96, [[D, P], [128, 16], [2, 16]])
                xo = ap3(scrA2, 97, [[D, P], [128, 16], [2, 16]])
                swe = ap3(scrB2, 0, [[D, P], [32, 16], [2, 16]])
                swo = ap3(scrB2, 1, [[D, P], [32, 16], [2, 16]])
                swv = ap3(scrB2, 0, [[D, P], [32, 16], [1, 32]])
                k.cp("vector", swe, xo, waits=(tq32, bankfree[1][0]))
                tsw = k.cp("vector", swo, xe, inc="s_dve")
                k.tt("vector", scrB2[:, 512:1024], xv, rqc[:],
                     ALU.mult, waits=(trc,))
                k.tt("vector", swv, swv, rqs[:], ALU.mult, waits=(trs, tsw))
                trq = k.tt("vector", xv, scrB2[:, 512:1024], swv, ALU.add,
                           inc="s_dve")
                tok_rq_free = trq
                tqb = k.cp("vector", dumpb[:], scrA2[:],
                           waits=(trq, vecfree[0], k.tok.get("p2qtr")),
                           inc="s_dve")
                vecfree[0] = None
                for g in range(NKV):
                    trt = None
                    for hh in range(4):
                        h = g * 4 + hh
                        trt = k.tr(ptrw[:, hh * 128:(hh + 1) * 128],
                                   dumpb[:, h * 128:(h + 1) * 128], iden[:],
                                   waits=[tqb] + tok_ptrw_free
                                   + tok_q4_free[g],
                                   inc="s_pe" if hh == 3 else None)
                    tev = k.cp("vector",
                               Q4[g][:, kin * 512:(kin + 1) * 512],
                               ptrw[:, 0:512], waits=(trt,), inc="s_dve")
                    tok_ptrw_free.clear()
                    tok_ptrw_free.append(tev)
                    tok_q4_free[g] = []
                    t_q4_last = tev
                k.tok["p2qtr"] = ("s_pe", k.cnt["s_pe"])

            tprev = vecfree[0]
            firstq = True
            for g in range(NKV):
                tsqq = k.act(dumpb[:], Q4[g][:], AF.Square,
                             waits=(t_q4_last, tprev), inc="s_act")
                for ch in range(4):
                    mm_ = k.mm(pbc[0:1, 0:512], onescb[:],
                               dumpb[:, ch * 512:(ch + 1) * 512],
                               True, True,
                               waits=[tsqq, tC["onescb"], tprev]
                               + tok_pbc_free, inc="s_pe")
                    tred_ = k.red(sca[0:1, 11:12], pbc[0:1, 0:512],
                                  ALU.max, waits=(mm_,), inc="s_dve")
                    tok_pbc_free = [tred_]
                    if firstq:
                        tprev = k.cp("vector", sca[0:1, 9:10],
                                     sca[0:1, 11:12], inc="s_dve")
                        firstq = False
                    else:
                        tprev = k.tt("vector", sca[0:1, 9:10],
                                     sca[0:1, 9:10], sca[0:1, 11:12],
                                     ALU.max, inc="s_dve")
            vecfree[0] = tprev
            k.tt("vector", sca[0:1, 10:11], sca[0:1, 8:9], sca[0:1, 9:10],
                 ALU.mult, waits=(tmk2, tprev))
            tln_ = k.act(sca[0:1, 10:11], sca[0:1, 10:11], AF.Ln,
                         waits=(("s_dve", k.cnt["s_dve"]),), inc="s_act")
            texp_ = k.act(sca[0:1, 10:11], sca[0:1, 10:11], AF.Exp,
                          scale=0.5, inc="s_act")
            tnb = k.ts("vector", sca[0:1, 10:11], sca[0:1, 10:11],
                       -CSC, None, ALU.mult, None, waits=(texp_,),
                       inc="s_dve")
            mb = k.mm(pbc[:, 0:2], onesrf[:], sca[0:1, 10:12], True, True,
                      waits=[tnb, tC["onesrf"]] + tok_pbc_free, inc="s_pe")
            tnegb = k.cp("vector", negB[:], pbc[:, 0:1],
                         waits=(mb, tok_negb_rd), inc="s_dve")
            tok_pbc_free = [tnegb]

            for kin in range(4):
                kown = quad * 4 + kin
                nkt = 4 * (kown + 1)
                for g in range(NKV):
                    tpv = None
                    first = True

                    def emit_S(kt):
                        spar = kt % 2
                        return k.mm(pso[spar][:],
                                    Kf[g][:, kt * P:(kt + 1) * P],
                                    Q4[g][:, kin * 512:(kin + 1) * 512],
                                    True, True,
                                    waits=[tnegb, tok_kdone, t_q4_last]
                                    + bankfree[spar],
                                    inc="s_pe")

                    tsm_next = emit_S(0)
                    for kt in range(nkt):
                        spar = kt % 2
                        epar = kt % 2
                        tsm = tsm_next
                        texp = k.act(E4[epar][:], pso[spar][:], AF.Exp,
                                     scale=CSC, bias=negB[:],
                                     waits=(tsm, tok_E_free[epar]),
                                     inc="s_act")
                        bankfree[spar] = [texp]
                        if kt + 1 < nkt:
                            tsm_next = emit_S(kt + 1)
                        tE = texp
                        j = kt - 4 * kown
                        if j >= 0:
                            tE = k.tt("vector", E4[epar][:], E4[epar][:],
                                      masks[:, j * 512:(j + 1) * 512],
                                      ALU.mult, waits=(texp, t_masks),
                                      inc="s_dve")
                        k.mm(plq[:], onescb[:], E4[epar][:],
                             first, kt == nkt - 1,
                             waits=[tE, tC["onescb"]]
                             + (tok_plq_free if first else []))
                        k.mm(pso[2][:], Vt[g][:, kt * VD:kt * VD + 128],
                             E4[epar][:], first, kt == nkt - 1,
                             waits=(bankfree[2] + bankfree[3])
                             if first else ())
                        tpv = k.mm(pso[3][:],
                                   Vt[g][:, kt * VD + 128:kt * VD + 256],
                                   E4[epar][:], first, kt == nkt - 1,
                                   inc="s_pe")
                        tok_E_free[epar] = tpv
                        first = False
                    tok_negb_rd = ("s_act", k.cnt["s_act"])
                    tlc = k.cp("vector", l32[:], plq[:], waits=(tpv,),
                               inc="s_dve")
                    tok_plq_free = [tlc]
                    k.recip(l32[:], l32[:])
                    trcl = k.cp("vector", reclb[:], l32[:], inc="s_dve")
                    mb2 = k.mm(pbc[:], onesrb[:], reclb[:], True, True,
                               waits=[trcl, tC["onesrb"]] + tok_pbc_free,
                               inc="s_pe")
                    tbc = k.cp("vector", bcb[:], pbc[:], waits=(mb2,),
                               inc="s_dve")
                    tok_pbc_free = [tbc]
                    opar = 0
                    dst0 = ap3(outfb[opar], g * 8 * 128,
                               [[H * VD, P], [256, 4], [1, 128]])
                    dst1 = ap3(outfb[opar], (g * 8 + 1) * 128,
                               [[H * VD, P], [256, 4], [1, 128]])
                    k.tt("vector", dst0, pso[2][:], bcb[:], ALU.mult,
                         waits=(tbc, tok_outf_free[opar]))
                    tev = k.tt("vector", dst1, pso[3][:], bcb[:], ALU.mult,
                               inc="s_dve")
                    bankfree[2] = [tev]
                    bankfree[3] = [tev]
                    tok_q4_free[g].append(tev)
                    if g == 3:
                        tok_outf_free[opar] = k.dma(
                            "gpsimd",
                            outf_dram[kown * P:(kown + 1) * P, :],
                            outfb[opar][:], waits=(tev,), sem="s_of")
                        k.tok[f"outf_dma{opar}"] = tok_outf_free[opar]
                    quant_slot()
        if dbg:
            k.dma("gpsimd", dbg_kv[:, 0:4096], Kf[0][:],
                  waits=[("s_pe", k.cnt["s_pe"]), ("s_dve", k.cnt["s_dve"])])
            k.dma("gpsimd", dbg_kv[:, 4096:4096 + 8192], Vt[0][:])
            k.dma("gpsimd", dbg_kv[:, 12288:14336], Q4[0][:])
            k.dma("gpsimd", dbg_kv[:, 14336:14848], E4[0][:])
            k.dma("gpsimd", dbg_kv[:, 14848:15360], bcb[:])
        k.barrier()
        k.flush(blk)
        s23.close()
        s12.close()
        s13.close()
        if stop <= 2:
            return nc

        s4 = contextlib.ExitStack()
        otb = [mk(s4, f"otb{i}", (P, H * VD), BF16) for i in range(4)]
        wob = [mk(s4, f"wob{i}", (P, 8 * D), BF16) for i in range(2)]
        xres = [mk(s4, f"xres{i}", (P, D), F32) for i in range(2)]
        yb = [mk(s4, f"yb{i}", (P, D), F32) for i in range(2)]
        scrA4 = mk(s4, "scrA4", (P, D), F32)
        pw = [s4.enter_context(nc.psum_tensor(f"pw{i}", [P, 512], F32))
              for i in range(8)]

        tok_ot_free = [None] * 4
        tok_wo_free = [None, None]
        tok_xr_free = [None, None]
        tok_y_free = [None, None]
        tok_pw_free = [[] for _ in range(8)]
        tok_x1_free = None

        for pair in range(4):
            ta, tb_ = pair * 2, pair * 2 + 1
            ia, ib = (pair % 2) * 2, (pair % 2) * 2 + 1
            tld_a = k.dma("scalar", otb[ia][:],
                          outf_dram[ta * P:(ta + 1) * P, :],
                          waits=(tok_ot_free[ia], k.tok.get("outf_dma0"),
                                 k.tok.get("outf_dma1")))
            tld_b = k.dma("scalar", otb[ib][:],
                          outf_dram[tb_ * P:(tb_ + 1) * P, :],
                          waits=(tok_ot_free[ib],))
            txa = k.dma("sync", xres[0][:], x_own[ta * P:(ta + 1) * P, :],
                        waits=(tok_xr_free[0],))
            txb = k.dma("sync", xres[1][:], x_own[tb_ * P:(tb_ + 1) * P, :],
                        waits=(tok_xr_free[1],))
            mml = None
            for hg in range(4):
                wpar = hg % 2
                wsrc = ap3(wo_bf_dram.tensor, hg * 8 * P * D,
                           [[D, P], [P * D, 8], [1, D]])
                twl = k.dma("sync", wob[wpar][:], wsrc,
                            waits=(tok_wo_free[wpar], t_wocast),
                            sem=f"s_wo{wpar}")
                for hh4 in range(8):
                    hv = hg * 8 + hh4
                    for ti, buf, tld in ((0, otb[ia], tld_a),
                                         (1, otb[ib], tld_b)):
                        for dc in range(4):
                            mml = k.mm(
                                pw[ti * 4 + dc][:],
                                buf[:, hv * 128:(hv + 1) * 128],
                                wob[wpar][:, hh4 * D + dc * 512:
                                           hh4 * D + (dc + 1) * 512],
                                hv == 0, hv == 31,
                                waits=[twl, tld]
                                + (tok_pw_free[ti * 4 + dc]
                                   if hv == 0 else []),
                                inc="s_pe" if (ti, dc) == (1, 3) else None)
                tok_wo_free[wpar] = ("s_pe", k.cnt["s_pe"])
            tok_ot_free[ia] = ("s_pe", k.cnt["s_pe"])
            tok_ot_free[ib] = tok_ot_free[ia]
            for ti in range(2):
                tglob = ta if ti == 0 else tb_
                txr = txa if ti == 0 else txb
                tlast = None
                for dc in range(4):
                    tlast = k.stt(
                        "vector", yb[ti][:, dc * 512:(dc + 1) * 512],
                        pw[ti * 4 + dc][:], 1.0,
                        xres[ti][:, dc * 512:(dc + 1) * 512],
                        ALU.mult, ALU.add,
                        waits=(mml, txr, tok_y_free[ti]), inc="s_dve")
                for dc in range(4):
                    tok_pw_free[ti * 4 + dc] = [tlast]
                tok_xr_free[ti] = tlast
                tvar, tapp, tx1 = ln_chain(yb[ti][:], D, None, scrA4[:],
                                           tlast, extra=(tok_x1_free,))
                tok_y_free[ti] = tapp
                tok_x1_free = k.dma("gpsimd",
                                    x1_dram[tglob * P:(tglob + 1) * P, :],
                                    scrA4[:], waits=(tx1,), sem="s_of")
        k.barrier()
        k.flush(blk)
        s4.close()
        if stop <= 3:
            return nc

        s5 = contextlib.ExitStack()
        hbuf = mk(s5, "hbuf", (P, OWN * IFF), BF16)
        scrA5 = mk(s5, "scrA5", (P, D), F32)
        scrI5 = scrA5.bitcast(I32)
        pup = [s5.enter_context(nc.psum_tensor(f"pup{i}", [P, 512], F32))
               for i in range(8)]
        s5a = contextlib.ExitStack()
        xqf = mk(s5a, "xqf", (P, OWN * 16 * 128), BF16)
        x1b = [mk(s5a, f"x1b{i}", (P, D), F32) for i in range(2)]
        wqs = [mk(s5a, f"wqs{i}", (P, 16 * 512), BF16) for i in range(2)]

        pupfree = {i: [] for i in range(8)}
        tok_x1b_free = [None, None]
        for t in range(OWN):
            par = t % 2
            tl = k.dma("scalar", x1b[par][:], x1_dram[t * P:(t + 1) * P, :],
                       waits=(tok_x1b_free[par], tok_x1_free))
            k.red(vec[:, 0:1], x1b[par][:], ALU.max, absval=True,
                  waits=(tl, vecfree[0]))
            k.ts("vector", vec[:, 0:1], vec[:, 0:1], float(EPS), None,
                 ALU.max, None)
            k.recip(vec[:, 1:2], vec[:, 0:1])
            tsc = k.ts("vector", vec[:, 1:2], vec[:, 1:2], 127.0, None,
                       ALU.mult, None, inc="s_dve")
            k.tt("vector", vec[:, 2:3], vec[:, 0:1], supv[:], ALU.mult,
                 waits=(t_scal,))
            tsr = k.ts("vector", srow1[:, t:t + 1], vec[:, 2:3],
                       1.0 / 127.0, None, ALU.mult, None, inc="s_dve")
            tq = k.act(scrA5[:], x1b[par][:], AF.Copy, scale=vec[:, 1:2],
                       waits=(tsc, k.tok.get("scrA5"),
                              k.tok.get("scrA5b")), inc="s_act")
            vecfree[0] = tq
            tok_x1b_free[par] = tq
            ti_ = k.cp("vector", scrI5[:], scrA5[:], waits=(tq,),
                       inc="s_dve")
            tcl = k.ts("vector", scrA5[:], scrI5[:], 127.0, -127.0, ALU.min,
                       ALU.max, waits=(ti_,), inc="s_dve")
            for pk in range(4):
                bk = pup[pk % 2]
                trt = None
                for bnum in range(4):
                    dc = pk * 4 + bnum
                    trt = k.tr(bk[:, bnum * 128:(bnum + 1) * 128],
                               scrA5[:, dc * 128:(dc + 1) * 128], idenf[:],
                               waits=[tcl] + pupfree[pk % 2],
                               inc="s_pe" if bnum == 3 else None)
                tev = k.cp(
                    "vector",
                    xqf[:, (t * 16 + pk * 4) * 128:
                        (t * 16 + pk * 4 + 4) * 128],
                    bk[:], waits=(trt,), inc="s_dve")
                pupfree[pk % 2] = [tev]
            k.tok["scrA5"] = ("s_pe", k.cnt["s_pe"])
        t_xqf = pupfree[1][0]

        wupT_r = wupT_d.rearrange("(dc p) i -> p dc i", p=P)
        tok_wqs_free = [None, None]
        t_quant_done = ("s_st", k.cnt["s_st"])
        t_h = None
        for ic in range(8):
            par = ic % 2
            tl = k.dma("sync", wqs[par][:],
                       wupT_r[:, :, ic * 512:(ic + 1) * 512],
                       waits=(tok_wqs_free[par], t_quant_done),
                       sem=f"s_wo{par}")
            mml = None
            for dc in range(16):
                for qs in range(8):
                    mml = k.mm(
                        pup[qs][:],
                        xqf[:, (qs * 16 + dc) * 128:
                            (qs * 16 + dc + 1) * 128],
                        wqs[par][:, dc * 512:(dc + 1) * 512],
                        dc == 0, dc == 15,
                        waits=[tl, t_xqf]
                        + (pupfree[qs] if dc == 0 else []),
                        inc="s_pe" if (dc, qs) == (15, 7) else None)
            tok_wqs_free[par] = ("s_pe", k.cnt["s_pe"])
            for qs in range(8):
                trl = k.act(scrA5[:, 0:512], pup[qs][:], AF.Relu,
                            scale=srow1[:, qs:qs + 1],
                            waits=(mml, tsr, k.tok.get("scrA5b")),
                            inc="s_act")
                pupfree[qs] = [trl]
                t_h = k.tt("vector",
                           hbuf[:, qs * IFF + ic * 512:
                                qs * IFF + (ic + 1) * 512],
                           scrA5[:, 0:512], scrA5[:, 0:512], ALU.mult,
                           waits=(trl,), inc="s_dve")
                k.tok["scrA5b"] = t_h

        t_hq = None
        for qs in range(8):
            k.red(vec[:, 0:1], hbuf[:, qs * IFF:(qs + 1) * IFF],
                  ALU.max, absval=True, waits=(t_h, vecfree[0]))
            k.ts("vector", vec[:, 0:1], vec[:, 0:1], float(EPS), None,
                 ALU.max, None)
            k.recip(vec[:, 1:2], vec[:, 0:1])
            tsc = k.ts("vector", vec[:, 1:2], vec[:, 1:2], 127.0, None,
                       ALU.mult, None, inc="s_dve")
            k.tt("vector", vec[:, 2:3], vec[:, 0:1], sdnv[:], ALU.mult)
            tsr2 = k.ts("vector", srow2[:, qs:qs + 1], vec[:, 2:3],
                        1.0 / 127.0, None, ALU.mult, None, inc="s_dve")
            for half in range(2):
                src = hbuf[:, qs * IFF + half * D:qs * IFF + (half + 1) * D]
                tq = k.act(scrA5[:], src, AF.Copy, scale=vec[:, 1:2],
                           waits=(tsc, k.tok.get("scrA5")), inc="s_act")
                vecfree[0] = tq
                ti_ = k.cp("vector", scrI5[:], scrA5[:], waits=(tq,),
                           inc="s_dve")
                tcl = k.ts("vector", scrA5[:], scrI5[:], 127.0, -127.0,
                           ALU.min, ALU.max, waits=(ti_,), inc="s_dve")
                for pk in range(4):
                    bk = pup[pk % 2]
                    trt = None
                    for bnum in range(4):
                        dcl = pk * 4 + bnum
                        trt = k.tr(bk[:, bnum * 128:(bnum + 1) * 128],
                                   scrA5[:, dcl * 128:(dcl + 1) * 128],
                                   idenf[:],
                                   waits=[tcl] + pupfree[pk % 2],
                                   inc="s_pe" if bnum == 3 else None)
                    dst = hbuf[:, qs * IFF + half * D + pk * 512:
                               qs * IFF + half * D + (pk + 1) * 512]
                    t_hq = k.cp("vector", dst, bk[:], waits=(trt, tq),
                                inc="s_dve")
                    pupfree[pk % 2] = [t_hq]
                k.tok["scrA5"] = ("s_pe", k.cnt["s_pe"])
        k.barrier()
        k.flush(blk)
        s5a.close()
        if stop <= 4:
            s5.close()
            return nc

        y2 = [mk(s5, f"y2_{i}", (P, D), F32) for i in range(4)]
        x1c = [mk(s5, f"x1c{i}", (P, 1024), F32) for i in range(2)]
        wdnb = [mk(s5, f"wdnb{i}", (P, 4096), BF16) for i in range(2)]
        pdn = pup
        pdnfree = {i: list(pupfree.get(i % 2, [])) for i in range(8)}
        tok_y2_free = [None] * 4
        for qq in range(2):
            for half in range(2):
                mml = None
                for icg in range(8):
                    par = icg % 2
                    wsrc = ap3(wdnT_d, icg * 512 * D + half * 1024,
                               [[D, P], [P * D, 4], [1, 1024]])
                    tl = k.dma("sync", wdnb[par][:], wsrc,
                               waits=(k.tok.get(f"wdn_free{par}"), t_hq),
                               sem=f"s_wo{par}")
                    for ic4 in range(4):
                        ic = icg * 4 + ic4
                        for qs in range(4):
                            qsub = qq * 4 + qs
                            for hh in range(2):
                                mml = k.mm(
                                    pdn[qs * 2 + hh][:],
                                    hbuf[:, qsub * IFF + ic * 128:
                                         qsub * IFF + (ic + 1) * 128],
                                    wdnb[par][:, ic4 * 1024 + hh * 512:
                                              ic4 * 1024 + (hh + 1) * 512],
                                    ic == 0, ic == 31,
                                    waits=[tl, t_hq]
                                    + (pdnfree[qs * 2 + hh]
                                       if ic == 0 else []),
                                    inc="s_pe" if (qs, hh, ic4) == (3, 1, 3)
                                    else None)
                    k.tok[f"wdn_free{par}"] = ("s_pe", k.cnt["s_pe"])
                for qs in range(4):
                    qsub = qq * 4 + qs
                    par = qs % 2
                    tlx = k.dma("scalar", x1c[par][:],
                                x1_dram[qsub * P:(qsub + 1) * P,
                                        half * 1024:(half + 1) * 1024],
                                waits=(k.tok.get(f"x1c_free{par}"),))
                    tev = None
                    for hh in range(2):
                        tev = k.stt(
                            "vector",
                            y2[qs][:, half * 1024 + hh * 512:
                                   half * 1024 + (hh + 1) * 512],
                            pdn[qs * 2 + hh][:], srow2[:, qsub:qsub + 1],
                            x1c[par][:, hh * 512:(hh + 1) * 512],
                            ALU.mult, ALU.add,
                            waits=(mml, tlx, tsr2, tok_y2_free[qs]),
                            inc="s_dve")
                    pdnfree[qs * 2] = [tev]
                    pdnfree[qs * 2 + 1] = [tev]
                    k.tok[f"x1c_free{par}"] = tev
                    if half == 1:
                        k.tok[f"y2rdy{qs}"] = tev
            for qs in range(4):
                qsub = qq * 4 + qs
                tyev = k.tok[f"y2rdy{qs}"]
                tvar, tapp, tout = ln_chain(y2[qs][:], D, None, scrA5[:],
                                            tyev,
                                            extra=(k.tok.get("out_free"),
                                                   k.tok.get("scrA5"),))
                tok_y2_free[qs] = tapp
                k.tok["scrA5"] = None
                k.tok["out_free"] = k.dma(
                    "gpsimd", out_own[qsub * P:(qsub + 1) * P, :],
                    scrA5[:], waits=(tout,), sem="s_of")
        if dbg:
            def _dbgdma(dst, src):
                def f(e):
                    return e.dma_start(out=dst, in_=src,
                                       allow_slow_non_contiguous=True)
                return f
            k.emit("gpsimd", _dbgdma(dbg_sc[:, 0:1], negB.ap()),
                   waits=[("s_of", k.cnt["s_of"]),
                          ("s_dve", k.cnt["s_dve"]),
                          ("s_act", k.cnt["s_act"]),
                          ("s_st", k.cnt["s_st"])], inc="s_st", inc_by=16)
            k.emit("gpsimd", _dbgdma(dbg_sc[:, 1:33], vec.ap()),
                   waits=[("s_st", k.cnt["s_st"])], inc="s_st", inc_by=16)
            k.emit("gpsimd", _dbgdma(dbg_sc[0:1, 33:65], sca.ap()),
                   waits=[("s_st", k.cnt["s_st"])], inc="s_st", inc_by=16)
            k.dma("gpsimd", dbg_x1, x1_dram)
            k.dma("gpsimd", dbg_of, outf_dram)
            k.dma("gpsimd", dbg_wu, wupT_d.ap())
        k.emit("gpsimd", lambda e: None,
               waits=tuple((sn, k.cnt[sn]) for sn in
                           ("s_st", "s_of", "s_stg")), chain=False)
        k.flush(blk)
        s5.close()

    return nc


def _host_inputs(inputs, core):
    x = np.ascontiguousarray(inputs["L0_attn_x"], dtype=np.float32)
    s, cc = core // 4, core % 4
    xs = np.ascontiguousarray(x[s])
    xown = np.ascontiguousarray(
        xs.reshape(NT, P, D)[cc::4].reshape(OWN * P, D))

    w_dq = np.asarray(inputs["L0_attn_w_dq"], dtype=np.float32)
    w_dkv = np.asarray(inputs["L0_attn_w_dkv"], dtype=np.float32)
    w_uq = np.asarray(inputs["L0_attn_w_uq"], dtype=np.float32)
    wdq_t = np.ascontiguousarray(
        w_dq.reshape(16, P, CQ).transpose(1, 0, 2).reshape(P, 16 * CQ))
    wdkv_t = np.ascontiguousarray(
        w_dkv.reshape(16, P, CKV + KR).transpose(1, 0, 2)
        .reshape(P, 16 * (CKV + KR)))
    perm = np.concatenate(
        [np.concatenate([np.arange(96 * h, 96 * h + 96),
                         1536 + np.arange(32 * h, 32 * h + 32)])
         for h in range(H)])
    wuq_p = np.ascontiguousarray(w_uq[:, perm])

    gq = np.ascontiguousarray(np.broadcast_to(
        np.asarray(inputs["L0_attn_gamma_q"], np.float32)[None, :], (P, CQ)))
    gkv = np.ascontiguousarray(np.broadcast_to(
        np.asarray(inputs["L0_attn_gamma_kv"], np.float32)[None, :],
        (P, CKV)))

    def rope_cs(dd):
        inv = (1.0 / THETA ** (np.arange(0, dd, 2, dtype=np.float32) / dd)
               ).astype(np.float32)
        ang = (np.arange(S, dtype=np.float32)[:, None] * inv[None, :]
               ).astype(np.float32)
        return (np.cos(ang.astype(np.float64)).astype(np.float32),
                np.sin(ang.astype(np.float64)).astype(np.float32))

    def swapform(c, s_):
        n = c.shape[1]
        C2 = np.empty((S, 2 * n), np.float32)
        S2 = np.empty((S, 2 * n), np.float32)
        C2[:, 0::2] = c
        C2[:, 1::2] = c
        S2[:, 0::2] = -s_
        S2[:, 1::2] = s_
        return C2, S2

    ck, sk = rope_cs(KR)
    C2k, S2k = swapform(ck, sk)
    ropeKc = np.ascontiguousarray(
        C2k.reshape(NT, P, KR).transpose(1, 0, 2).reshape(P, NT * KR))
    ropeKs = np.ascontiguousarray(
        S2k.reshape(NT, P, KR).transpose(1, 0, 2).reshape(P, NT * KR))

    cq_, sq_ = rope_cs(QR)
    C2q, S2q = swapform(cq_, sq_)
    own_pos = (np.arange(NT)[cc::4][:, None] * P
               + np.arange(P)[None, :]).reshape(-1)
    C2qo = np.ascontiguousarray(np.tile(C2q[own_pos], (1, H)))
    S2qo = np.ascontiguousarray(np.tile(S2q[own_pos], (1, H)))

    tri = (np.arange(P)[:, None] <= np.arange(P)[None, :]).astype(np.float32)
    m = np.zeros((P, 4 * 512), np.float32)
    for j in range(4):
        pat = (np.ones((P, P), np.float32) if j < cc
               else (tri if j == cc else np.zeros((P, P), np.float32)))
        for hh in range(4):
            m[:, j * 512 + hh * P:j * 512 + (hh + 1) * P] = pat

    return {
        "x_seq": xs,
        "x_own": xown,
        "wdq_t": wdq_t,
        "wdkv_t": wdkv_t,
        "wuq_p": wuq_p,
        "wukv_i": np.ascontiguousarray(
            np.asarray(inputs["L0_attn_w_ukv"], np.float32)),
        "w_o": np.ascontiguousarray(
            np.asarray(inputs["L0_attn_w_o"], np.float32)),
        "w_up": np.ascontiguousarray(
            np.asarray(inputs["L0_ffn_w_up"], np.float32)),
        "w_down": np.ascontiguousarray(
            np.asarray(inputs["L0_ffn_w_down"], np.float32)),
        "gq_bc": gq,
        "gkv_bc": gkv,
        "ropeKc": ropeKc,
        "ropeKs": ropeKs,
        "ropeQc": C2qo,
        "ropeQs": S2qo,
        "mask4": m.astype(BFD),
        "iden_f": np.eye(P).astype(BFD),
        "iden_f32": np.eye(P).astype(np.float32),
        "ones_cb": np.ones((P, 1), BFD),
        "ones_rb": np.ones((1, P), BFD),
        "ones_cf": np.ones((P, 1), np.float32),
        "ones_rf": np.ones((1, P), np.float32),
    }


def kernel(**inputs):
    nc = build_program()
    in_maps = [_host_inputs(inputs, c) for c in range(8)]
    res = run_bass_kernel_spmd(nc, in_maps, core_ids=list(range(8)))
    out = np.empty((B, S, D), np.float32)
    for c in range(8):
        s, cc = c // 4, c % 4
        o = np.asarray(res.results[c]["out_own"]).reshape(OWN, P, D)
        out[s].reshape(NT, P, D)[cc::4] = o
    return out
